# version 03
# baseline (speedup 1.0000x reference)
import numpy as np
import ml_dtypes
from contextlib import ExitStack

import concourse.bass as bass
import concourse.mybir as mybir
from concourse.bass_utils import run_bass_kernel_spmd

F32 = mybir.dt.float32
BF16 = mybir.dt.bfloat16
F16 = mybir.dt.float16
I32 = mybir.dt.int32
U16 = mybir.dt.uint16
U8 = mybir.dt.uint8
AF = mybir.ActivationFunctionType
ALU = mybir.AluOpType
BF = ml_dtypes.bfloat16

NC = 8
D = 2048
NI = 5632
NH = 16
HD = 128
SEQ = 4096
NB = 2
LN_EPS = 1e-5
RMS_EPS = 1e-6
SM_SCALE = 1.0 / np.sqrt(128.0)



class BlkMgr:

    def __init__(self, nc):
        self.nc = nc
        self.cm = None

    def new(self, name):
        if self.cm is not None:
            self.cm.__exit__(None, None, None)
        self.cm = self.nc.Block(name)
        return self.cm.__enter__()

    def close(self):
        if self.cm is not None:
            self.cm.__exit__(None, None, None)
            self.cm = None


class Sem:
    def __init__(self, handle):
        self.h = handle
        self.v = 0

    def inc(self, amt=1):
        self.v += amt
        return self.v


def _mk(nc, ctx, names):
    return [Sem(ctx.enter_context(nc.semaphore(n))) for n in names]


def chain(eng, sem, fns):
    last = None
    for k, f in enumerate(fns):
        if k > 0:
            eng.wait_ge(sem.h, sem.v)
        ins = f()
        ins.then_inc(sem.h, 1)
        sem.inc(1)
        last = ins
    return last



def emit_dequant_tiles(nc, BLK, sems, tiles, bufs, wmax):
    s_in, s_dec, s_out = sems
    byt, scb, shc = bufs["byt"], bufs["scb"], bufs["shc"]
    nr8, n15, rb1, rb2, rm, ru, rv, bits, wt = (
        bufs["nr8"], bufs["n15"], bufs["rb1"], bufs["rb2"], bufs["rm"],
        bufs["ru"], bufs["rv"], bufs["bits"], bufs["wt"])

    for i, (g, w, b2, sc, out) in enumerate(tiles):
        p = i % 2
        blk = BLK.new(f"dq{i}")

        @blk.sync
        def _(sync, g=g, w=w, b2=b2, sc=sc, p=p, i=i, dec_prev=s_dec.v):
            if i >= 2:
                sync.wait_ge(s_dec.h, dec_prev)
            sync.dma_start(out=byt[p][:, 0:w], in_=b2[g * 128:(g + 1) * 128, :]).then_inc(s_in.h, 16)
            sync.dma_start(out=scb[p][:, 0:w],
                           in_=sc[g:g + 1, :].broadcast_to([128, w])).then_inc(s_in.h, 16)
        s_in.inc(32)
        loaded = s_in.v

        @blk.vector
        def _(vector, g=g, w=w, p=p, i=i, loaded=loaded, st_prev=s_out.v):
            vector.wait_ge(s_in.h, loaded)
            if i >= 2:
                vector.wait_ge(s_out.h, st_prev - 16 + 16)
            B = byt[p][:, 0:w]
            chain(vector, s_dec, [
                lambda: vector.tensor_scalar(out=nr8[:, 0:w], in0=B, scalar1=shc[:],
                                             scalar2=None, op0=ALU.logical_shift_right),
                lambda: vector.tensor_scalar(out=nr8[:, 0:w], in0=nr8[:, 0:w], scalar1=15,
                                             scalar2=None, op0=ALU.bitwise_and),
                lambda: vector.tensor_copy(out=n15[:, 0:w], in_=nr8[:, 0:w]),
                lambda: vector.tensor_scalar(out=rb1[:, 0:w], in0=n15[:, 0:w], scalar1=8, scalar2=12,
                                             op0=ALU.bitwise_and, op1=ALU.logical_shift_left),
                lambda: vector.tensor_scalar(out=rb2[:, 0:w], in0=n15[:, 0:w], scalar1=7, scalar2=6,
                                             op0=ALU.bitwise_and, op1=ALU.logical_shift_left),
                lambda: vector.tensor_scalar(out=rm[:, 0:w], in0=n15[:, 0:w], scalar1=6,
                                             scalar2=None, op0=ALU.bitwise_and),
                lambda: vector.tensor_scalar(out=ru[:, 0:w], in0=rm[:, 0:w], scalar1=0, scalar2=16128,
                                             op0=ALU.is_equal, op1=ALU.mult),
                lambda: vector.tensor_scalar(out=rm[:, 0:w], in0=n15[:, 0:w], scalar1=7,
                                             scalar2=None, op0=ALU.bitwise_and),
                lambda: vector.tensor_scalar(out=rv[:, 0:w], in0=rm[:, 0:w], scalar1=1, scalar2=16064,
                                             op0=ALU.is_equal, op1=ALU.mult),
                lambda: vector.scalar_tensor_tensor(out=rb1[:, 0:w], in0=rb1[:, 0:w], scalar=16128.0,
                                                    in1=rb2[:, 0:w], op0=ALU.add, op1=ALU.add),
                lambda: vector.tensor_tensor(out=rb1[:, 0:w], in0=rb1[:, 0:w], in1=ru[:, 0:w],
                                             op=ALU.subtract),
                lambda: vector.tensor_tensor(out=bits[:, 0:w], in0=rb1[:, 0:w], in1=rv[:, 0:w],
                                             op=ALU.add),
                lambda: vector.tensor_tensor(out=wt[p][:, 0:w], in0=bits[:, 0:w].bitcast(BF16),
                                             in1=scb[p][:, 0:w], op=ALU.mult),
            ])
        dec = s_dec.v

        @blk.gpsimd
        def _(gpsimd, g=g, w=w, out=out, p=p, dec=dec):
            gpsimd.wait_ge(s_dec.h, dec)
            gpsimd.dma_start(out=out[g * 128:(g + 1) * 128, :], in_=wt[p][:, 0:w]).then_inc(s_out.h, 16)
        s_out.inc(16)


def alloc_dequant_bufs(nc, E, wmax):
    bufs = {}
    bufs["byt"] = [E(nc.sbuf_tensor(f"byt{i}", [128, wmax], U8)) for i in range(2)]
    bufs["scb"] = [E(nc.sbuf_tensor(f"scb{i}", [128, wmax], F16)) for i in range(2)]
    bufs["shc"] = E(nc.sbuf_tensor("shc", [128, 1], U8))
    bufs["nr8"] = E(nc.sbuf_tensor("nr8", [128, wmax], U8))
    for nm in ("n15", "rb1", "rb2", "rm", "ru", "rv"):
        bufs[nm] = E(nc.sbuf_tensor(nm, [128, wmax], I32))
    bufs["bits"] = E(nc.sbuf_tensor("bits", [128, wmax], U16))
    bufs["wt"] = [E(nc.sbuf_tensor(f"wtile{i}", [128, wmax], BF16)) for i in range(2)]
    return bufs


def build_dequant(gw, dw):
    nc = bass.Bass("TRN2", target_bir_lowering=False)
    b2g = nc.dram_tensor("b2g", [D, gw], U8, kind="ExternalInput").ap()
    sg = nc.dram_tensor("sg", [16, gw], F16, kind="ExternalInput").ap()
    b2d = nc.dram_tensor("b2d", [NI, dw], U8, kind="ExternalInput").ap()
    sd = nc.dram_tensor("sd", [44, dw], F16, kind="ExternalInput").ap()
    shift_in = nc.dram_tensor("shift", [128, 1], U8, kind="ExternalInput").ap()
    wg_out = nc.dram_tensor("wg", [D, gw], BF16, kind="ExternalOutput").ap()
    wd_out = nc.dram_tensor("wd", [NI, dw], BF16, kind="ExternalOutput").ap()

    wmax = max(gw, dw)
    with ExitStack() as ctx:
        E = ctx.enter_context
        BLK = BlkMgr(nc)
        bufs = alloc_dequant_bufs(nc, E, wmax)
        s_in, s_dec, s_out = _mk(nc, ctx, ["s_in", "s_dec", "s_out"])

        blk = BLK.new("init")

        @blk.sync
        def _(sync):
            sync.dma_start(out=bufs["shc"][:], in_=shift_in).then_inc(s_in.h, 16)
        s_in.inc(16)

        tiles = [(g, gw, b2g, sg, wg_out) for g in range(16)]
        tiles += [(g, dw, b2d, sd, wd_out) for g in range(44)]
        emit_dequant_tiles(nc, BLK, (s_in, s_dec, s_out), tiles, bufs, wmax)

        blk = BLK.new("fin")
        final = s_out.v

        @blk.gpsimd
        def _(gpsimd):
            gpsimd.wait_ge(s_out.h, final)
        BLK.close()
    return nc



def emit_ln(nc, BLK, bufs, sems, t0, tw, tag):
    s_pe, s_dv, s_sc = sems
    x, xn = bufs["x"], bufs["xn"]
    xbf, sqb = bufs["xbf"], bufs["sqb"]
    rows = bufs["rows"]
    a_b, b_b, tmp = bufs["a_b"], bufs["b_b"], bufs["tmp"]
    psS1, psS2, psC, psD = bufs["psS1"], bufs["psS2"], bufs["psC"], bufs["psD"]
    ones_bf, ones_f32 = bufs["ones_bf"], bufs["ones_f32"]

    blk = BLK.new(f"ln_{tag}")

    @blk.vector
    def _(vector, dv_now=s_dv.v):
        vector.wait_ge(s_dv.h, dv_now)
        for c in range(16):
            ins = vector.tensor_copy(out=xbf[c][:, 0:tw], in_=x[c][:, t0:t0 + tw])
            if c == 15:
                ins.then_inc(s_dv.h, 1)
    s_dv.inc(1)
    cast_done = s_dv.v

    @blk.scalar
    def _(scalar):
        scalar.wait_ge(s_dv.h, cast_done)
        for c in range(16):
            ins = scalar.activation(out=sqb[c][:, 0:tw], in_=xbf[c][:, 0:tw], func=AF.Square)
            if c == 15:
                ins.then_inc(s_sc.h, 1)
    s_sc.inc(1)
    sq_done = s_sc.v

    @blk.tensor
    def _(tensor):
        tensor.wait_ge(s_sc.h, sq_done)
        for c in range(16):
            tensor.matmul(out=psS1[0:1, 0:tw], lhsT=ones_bf[:, 0:1], rhs=xbf[c][:, 0:tw],
                          start=(c == 0), stop=(c == 15))
        for c in range(16):
            ins = tensor.matmul(out=psS2[0:1, 0:tw], lhsT=ones_bf[:, 0:1], rhs=sqb[c][:, 0:tw],
                                start=(c == 0), stop=(c == 15))
            if c == 15:
                ins.then_inc(s_pe.h, 1)
    s_pe.inc(1)
    stats_done = s_pe.v

    blk = BLK.new(f"ln_{tag}_r")

    @blk.vector
    def _(vector):
        vector.wait_ge(s_pe.h, stats_done)
        chain(vector, s_dv, [
            lambda: vector.tensor_scalar(out=rows[0][:, 0:tw], in0=psS1[0:1, 0:tw],
                                         scalar1=1.0 / D, scalar2=None, op0=ALU.mult),
            lambda: vector.tensor_scalar(out=rows[1][:, 0:tw], in0=psS2[0:1, 0:tw],
                                         scalar1=1.0 / D, scalar2=None, op0=ALU.mult),
            lambda: vector.tensor_tensor(out=rows[2][:, 0:tw], in0=rows[0][:, 0:tw],
                                         in1=rows[0][:, 0:tw], op=ALU.mult),
            lambda: vector.tensor_tensor(out=rows[2][:, 0:tw], in0=rows[1][:, 0:tw],
                                         in1=rows[2][:, 0:tw], op=ALU.subtract),
        ])
    var_done = s_dv.v

    @blk.scalar
    def _(scalar):
        scalar.wait_ge(s_dv.h, var_done)
        chain(scalar, s_sc, [
            lambda: scalar.activation(out=rows[3][:, 0:tw], in_=rows[2][:, 0:tw], func=AF.Ln,
                                      bias=bufs["cst"][0:1, 0:1]),
            lambda: scalar.activation(out=rows[4][:, 0:tw], in_=rows[3][:, 0:tw], func=AF.Exp,
                                      scale=-0.5),
        ])
    rstd_done = s_sc.v

    @blk.vector
    def _(vector):
        vector.wait_ge(s_sc.h, rstd_done)
        vector.wait_ge(s_dv.h, s_dv.v)
        vector.tensor_tensor(out=rows[5][:, 0:tw], in0=rows[0][:, 0:tw], in1=rows[4][:, 0:tw],
                             op=ALU.mult).then_inc(s_dv.h, 1)
    s_dv.inc(1)
    mr_done = s_dv.v

    @blk.tensor
    def _(tensor):
        tensor.wait_ge(s_dv.h, mr_done)
        tensor.matmul(out=psC[:, 0:tw], lhsT=ones_f32[0:1, :], rhs=rows[4][:, 0:tw],
                      start=True, stop=True)
        tensor.matmul(out=psD[:, 0:tw], lhsT=ones_f32[0:1, :], rhs=rows[5][:, 0:tw],
                      start=True, stop=True).then_inc(s_pe.h, 1)
    s_pe.inc(1)
    bc_done = s_pe.v

    blk = BLK.new(f"ln_{tag}_a")

    @blk.vector
    def _(vector):
        vector.wait_ge(s_pe.h, bc_done)
        fns = [lambda: vector.tensor_copy(out=a_b[:, 0:tw], in_=psC[:, 0:tw]),
               lambda: vector.tensor_copy(out=b_b[:, 0:tw], in_=psD[:, 0:tw])]
        for c in range(16):
            fns.append(lambda c=c: vector.tensor_tensor(
                out=tmp[:, 0:tw], in0=x[c][:, t0:t0 + tw], in1=a_b[:, 0:tw], op=ALU.mult))
            fns.append(lambda c=c: vector.tensor_tensor(
                out=xn[c][:, t0:t0 + tw], in0=tmp[:, 0:tw], in1=b_b[:, 0:tw], op=ALU.subtract))
        chain(vector, s_dv, fns)
    return s_dv.v


def alloc_common(nc, E, T, with_h):
    bufs = {}
    bufs["x"] = [E(nc.sbuf_tensor(f"x{c}", [128, T], F32)) for c in range(16)]
    bufs["xn"] = [E(nc.sbuf_tensor(f"xn{c}", [128, T], BF16)) for c in range(16)]
    if with_h:
        bufs["h"] = [E(nc.sbuf_tensor(f"h{i}", [128, 512], BF16)) for i in range(44)]
        bufs["xbf"] = bufs["h"][0:16]
        bufs["sqb"] = bufs["h"][16:32]
        if T >= 1024:
            bufs["y"] = [bufs["xn"][c][:].bitcast(F32)[:, 0:512] for c in range(16)]
        else:
            bufs["y"] = [E(nc.sbuf_tensor(f"y{c}", [128, 512], F32))[:] for c in range(16)]
    else:
        bufs["xbf"] = [E(nc.sbuf_tensor(f"xbf{c}", [128, 512], BF16)) for c in range(16)]
        bufs["sqb"] = [E(nc.sbuf_tensor(f"sqb{c}", [128, 512], BF16)) for c in range(16)]
    bufs["rows"] = [E(nc.sbuf_tensor(f"row{i}", [1, 512], F32)) for i in range(8)]
    bufs["a_b"] = E(nc.sbuf_tensor("a_b", [128, 512], F32))
    bufs["b_b"] = E(nc.sbuf_tensor("b_b", [128, 512], F32))
    bufs["tmp"] = E(nc.sbuf_tensor("tmp", [128, 512], F32))
    bufs["ones_bf"] = E(nc.sbuf_tensor("ones_bf", [128, 2], BF16))
    bufs["ones_f32"] = E(nc.sbuf_tensor("ones_f32", [1, 128], F32))
    bufs["cst"] = E(nc.sbuf_tensor("cst", [1, 4], F32))
    bufs["psS1"] = E(nc.psum_tensor("psS1", [1, 512], F32))
    bufs["psS2"] = E(nc.psum_tensor("psS2", [1, 512], F32))
    bufs["psA"] = E(nc.psum_tensor("psA", [128, 512], F32))
    bufs["psB"] = E(nc.psum_tensor("psB", [128, 512], F32))
    bufs["psC"] = E(nc.psum_tensor("psC", [128, 512], F32))
    bufs["psD"] = E(nc.psum_tensor("psD", [128, 512], F32))
    return bufs



def emit_ffn_body(nc, BLK, bufs, sems, T, wgu, wdn, tag):
    s_ld, s_ld2, s_pe, s_dv, s_sc = sems
    x, xn = bufs["x"], bufs["xn"]
    rows = bufs["rows"]
    psS1, psA, psB, psC = bufs["psS1"], bufs["psA"], bufs["psB"], bufs["psC"]
    ones_bf, ones_f32 = bufs["ones_bf"], bufs["ones_f32"]
    wgc, wuc, wdc = bufs["wgc"], bufs["wuc"], bufs["wdc"]
    sg_bf, h, y = bufs["sg_bf"], bufs["h"], bufs["y"]
    sqy, rb, tmp = bufs["sqy"], bufs["rb"], bufs["tmp"]
    cst = bufs["cst"]

    n_tt = (T + 511) // 512
    for tt in range(n_tt):
        t0 = tt * 512
        tw = min(512, T - t0)
        xn_ready = emit_ln(nc, BLK, bufs, (s_pe, s_dv, s_sc), t0, tw, f"{tag}{tt}")

        for i in range(44):
            p = i % 2
            blk = BLK.new(f"gu_{tag}{tt}_{i}")

            @blk.sync
            def _(sync, i=i, p=p, pe_now=s_pe.v):
                if i >= 2:
                    sync.wait_ge(s_pe.h, pe_now)
                sync.dma_start(out=wgc[p][:], in_=wgu[i]).then_inc(s_ld.h, 16)
            s_ld.inc(16)
            g_loaded = s_ld.v

            @blk.gpsimd
            def _(gpsimd, i=i, p=p, pe_now=s_pe.v):
                if i >= 2:
                    gpsimd.wait_ge(s_pe.h, pe_now)
                gpsimd.dma_start(out=wuc[p][:], in_=wgu[44 + i]).then_inc(s_ld2.h, 16)
            s_ld2.inc(16)
            u_loaded = s_ld2.v

            @blk.tensor
            def _(tensor, i=i, p=p, g_loaded=g_loaded, u_loaded=u_loaded,
                  xn_ready=xn_ready, tw=tw, t0=t0, dv_now=s_dv.v, sc_now=s_sc.v):
                tensor.wait_ge(s_ld.h, g_loaded)
                tensor.wait_ge(s_ld2.h, u_loaded)
                if i == 0:
                    tensor.wait_ge(s_dv.h, xn_ready)
                else:
                    tensor.wait_ge(s_dv.h, dv_now)
                    tensor.wait_ge(s_sc.h, sc_now)
                for g in range(16):
                    tensor.matmul(out=psA[:, 0:tw], lhsT=wgc[p][:, g * 128:(g + 1) * 128],
                                  rhs=xn[g][:, t0:t0 + tw], start=(g == 0), stop=(g == 15))
                for g in range(16):
                    ins = tensor.matmul(out=psB[:, 0:tw], lhsT=wuc[p][:, g * 128:(g + 1) * 128],
                                        rhs=xn[g][:, t0:t0 + tw], start=(g == 0), stop=(g == 15))
                    if g == 15:
                        ins.then_inc(s_pe.h, 1)
            s_pe.inc(1)
            mm_done = s_pe.v

            @blk.scalar
            def _(scalar, mm_done=mm_done, tw=tw):
                scalar.wait_ge(s_pe.h, mm_done)
                scalar.activation(out=sg_bf[:, 0:tw], in_=psA[:, 0:tw], func=AF.Silu).then_inc(s_sc.h, 1)
            s_sc.inc(1)
            silu_done = s_sc.v

            @blk.vector
            def _(vector, i=i, silu_done=silu_done, tw=tw):
                vector.wait_ge(s_sc.h, silu_done)
                vector.tensor_tensor(out=h[i][:, 0:tw], in0=sg_bf[:, 0:tw], in1=psB[:, 0:tw],
                                     op=ALU.mult).then_inc(s_dv.h, 1)
            s_dv.inc(1)

        h_all_done = s_dv.v
        for m in range(16):
            p = m % 2
            ps = psA if p == 0 else psB
            blk = BLK.new(f"dn_{tag}{tt}_{m}")

            @blk.sync
            def _(sync, m=m, p=p, pe_now=s_pe.v):
                if m >= 2:
                    sync.wait_ge(s_pe.h, pe_now)
                sync.dma_start(out=wdc[p][:], in_=wdn[m]).then_inc(s_ld.h, 16)
            s_ld.inc(16)
            d_loaded = s_ld.v

            @blk.tensor
            def _(tensor, m=m, ps=ps, p=p, d_loaded=d_loaded, h_all_done=h_all_done,
                  tw=tw, dv_now=s_dv.v, sc_now=s_sc.v):
                tensor.wait_ge(s_ld.h, d_loaded)
                if m == 0:
                    tensor.wait_ge(s_dv.h, h_all_done)
                if m >= 2:
                    tensor.wait_ge(s_dv.h, dv_now)
                    tensor.wait_ge(s_sc.h, sc_now)
                for g in range(44):
                    ins = tensor.matmul(out=ps[:, 0:tw], lhsT=wdc[p][:, g * 128:(g + 1) * 128],
                                        rhs=h[g][:, 0:tw], start=(g == 0), stop=(g == 43))
                    if g == 43:
                        ins.then_inc(s_pe.h, 1)
            s_pe.inc(1)
            dmm_done = s_pe.v

            @blk.scalar
            def _(scalar, ps=ps, dmm_done=dmm_done, tw=tw):
                scalar.wait_ge(s_pe.h, dmm_done)
                scalar.activation(out=sqy[:, 0:tw], in_=ps[:, 0:tw], func=AF.Square).then_inc(s_sc.h, 1)
            s_sc.inc(1)
            sq_done = s_sc.v

            @blk.vector
            def _(vector, m=m, ps=ps, dmm_done=dmm_done, tw=tw):
                vector.wait_ge(s_pe.h, dmm_done)
                vector.tensor_copy(out=y[m][:, 0:tw], in_=ps[:, 0:tw]).then_inc(s_dv.h, 1)
            s_dv.inc(1)

            @blk.tensor
            def _(tensor, m=m, sq_done=sq_done, tw=tw):
                tensor.wait_ge(s_sc.h, sq_done)
                ins = tensor.matmul(out=psS1[0:1, 0:tw], lhsT=ones_bf[:, 0:1], rhs=sqy[:, 0:tw],
                                    start=(m == 0), stop=(m == 15))
                if m == 15:
                    ins.then_inc(s_pe.h, 1)
            if m == 15:
                s_pe.inc(1)
        ss_done = s_pe.v

        blk = BLK.new(f"rms_{tag}{tt}")

        @blk.scalar
        def _(scalar, ss_done=ss_done, tw=tw):
            scalar.wait_ge(s_pe.h, ss_done)
            chain(scalar, s_sc, [
                lambda: scalar.activation(out=rows[6][:, 0:tw], in_=psS1[0:1, 0:tw], func=AF.Ln,
                                          scale=1.0 / D, bias=cst[0:1, 1:2]),
                lambda: scalar.activation(out=rows[7][:, 0:tw], in_=rows[6][:, 0:tw], func=AF.Exp,
                                          scale=-0.5, bias=cst[0:1, 2:3]),
            ])
        rr_done = s_sc.v

        @blk.tensor
        def _(tensor, rr_done=rr_done, tw=tw):
            tensor.wait_ge(s_sc.h, rr_done)
            tensor.matmul(out=psC[:, 0:tw], lhsT=ones_f32[0:1, :], rhs=rows[7][:, 0:tw],
                          start=True, stop=True).then_inc(s_pe.h, 1)
        s_pe.inc(1)
        rbc_done = s_pe.v

        @blk.vector
        def _(vector, rbc_done=rbc_done, t0=t0, tw=tw, dv_now=s_dv.v):
            vector.wait_ge(s_pe.h, rbc_done)
            vector.wait_ge(s_dv.h, dv_now)
            fns = [lambda: vector.tensor_copy(out=rb[:, 0:tw], in_=psC[:, 0:tw])]
            for c in range(16):
                fns.append(lambda c=c: vector.tensor_tensor(
                    out=tmp[:, 0:tw], in0=y[c][:, 0:tw], in1=rb[:, 0:tw], op=ALU.mult))
                fns.append(lambda c=c: vector.tensor_tensor(
                    out=x[c][:, t0:t0 + tw], in0=x[c][:, t0:t0 + tw], in1=tmp[:, 0:tw], op=ALU.add))
            chain(vector, s_dv, fns)
    return s_dv.v


def build_ffn(T, with_attn_add):
    nc = bass.Bass("TRN2", target_bir_lowering=False)
    x_in = nc.dram_tensor("x_in", [D, T], F32, kind="ExternalInput").ap()
    wgu = nc.dram_tensor("wgu", [88, 128, 2048], BF16, kind="ExternalInput").ap()
    wdn = nc.dram_tensor("wdn", [16, 128, NI], BF16, kind="ExternalInput").ap()
    attn_in = None
    if with_attn_add:
        attn_in = nc.dram_tensor("attn_in", [D, T], F32, kind="ExternalInput").ap()
    x_out = nc.dram_tensor("x_out", [D, T], F32, kind="ExternalOutput").ap()

    with ExitStack() as ctx:
        E = ctx.enter_context
        BLK = BlkMgr(nc)
        bufs = alloc_common(nc, E, T, with_h=True)
        bufs["wgc"] = [E(nc.sbuf_tensor(f"wgc{i}", [128, 2048], BF16)) for i in range(2)]
        bufs["wuc"] = [E(nc.sbuf_tensor(f"wuc{i}", [128, 2048], BF16)) for i in range(2)]
        bufs["wdc"] = [E(nc.sbuf_tensor(f"wdc{i}", [128, NI], BF16)) for i in range(2)]
        bufs["sg_bf"] = E(nc.sbuf_tensor("sg_bf", [128, 512], BF16))
        bufs["sqy"] = E(nc.sbuf_tensor("sqy", [128, 512], BF16))
        bufs["rb"] = E(nc.sbuf_tensor("rb", [128, 512], F32))
        atmp = E(nc.sbuf_tensor("atmp", [128, 512], F32)) if with_attn_add else None
        s_ld, s_ld2, s_pe, s_dv, s_sc, s_st = _mk(
            nc, ctx, ["s_ld", "s_ld2", "s_pe", "s_dv", "s_sc", "s_st"])

        blk = BLK.new("load")

        @blk.sync
        def _(sync):
            for c in range(16):
                sync.dma_start(out=bufs["x"][c][:], in_=x_in[c * 128:(c + 1) * 128, :]).then_inc(s_ld.h, 16)
        s_ld.inc(16 * 16)
        x_loaded = s_ld.v

        if with_attn_add:
            for c in range(16):
                for hf in range(T // 512):
                    blk = BLK.new(f"aadd{c}_{hf}")

                    @blk.gpsimd
                    def _(gpsimd, c=c, hf=hf, dv_now=s_dv.v):
                        if c + hf >= 1:
                            gpsimd.wait_ge(s_dv.h, dv_now)
                        gpsimd.dma_start(out=atmp[:],
                                         in_=attn_in[c * 128:(c + 1) * 128, hf * 512:(hf + 1) * 512]
                                         ).then_inc(s_ld2.h, 16)
                    s_ld2.inc(16)
                    a_loaded = s_ld2.v

                    @blk.vector
                    def _(vector, c=c, hf=hf, a_loaded=a_loaded, x_loaded=x_loaded):
                        vector.wait_ge(s_ld2.h, a_loaded)
                        if c == 0 and hf == 0:
                            vector.wait_ge(s_ld.h, x_loaded)
                        vector.tensor_tensor(out=bufs["x"][c][:, hf * 512:(hf + 1) * 512],
                                             in0=bufs["x"][c][:, hf * 512:(hf + 1) * 512],
                                             in1=atmp[:], op=ALU.add).then_inc(s_dv.h, 1)
                    s_dv.inc(1)

        blk = BLK.new("init")

        @blk.vector
        def _(vector, x_loaded=x_loaded, dv_now=s_dv.v):
            if not with_attn_add:
                vector.wait_ge(s_ld.h, x_loaded)
            vector.memset(bufs["ones_f32"][:], 1.0)
            vector.memset(bufs["ones_bf"][:], 1.0)
            vector.memset(bufs["cst"][0:1, 0:1], LN_EPS)
            vector.memset(bufs["cst"][0:1, 1:2], RMS_EPS)
            vector.memset(bufs["cst"][0:1, 2:3], float(np.log(0.5))).then_inc(s_dv.h, 1)
        s_dv.inc(1)

        emit_ffn_body(nc, BLK, bufs, (s_ld, s_ld2, s_pe, s_dv, s_sc), T, wgu, wdn, "f")
        done = s_dv.v

        blk = BLK.new("store")

        @blk.gpsimd
        def _(gpsimd, done=done):
            gpsimd.wait_ge(s_dv.h, done)
            for c in range(16):
                gpsimd.dma_start(out=x_out[c * 128:(c + 1) * 128, :], in_=bufs["x"][c][:]).then_inc(s_st.h, 16)
            gpsimd.wait_ge(s_st.h, 16 * 16)
        s_st.inc(16 * 16)
        BLK.close()
    return nc



def build_qkv(T):
    nc = bass.Bass("TRN2", target_bir_lowering=False)
    x_in = nc.dram_tensor("x_in", [D, T], F32, kind="ExternalInput").ap()
    wqk = nc.dram_tensor("wqk", [32, 128, 2048], F32, kind="ExternalInput").ap()
    wv = nc.dram_tensor("wv", [8, 128, 4096], F32, kind="ExternalInput").ap()
    bqk = nc.dram_tensor("bqk", [128, 32], F32, kind="ExternalInput").ap()
    bv = nc.dram_tensor("bv", [1, D], F32, kind="ExternalInput").ap()
    tab_in = {nm: nc.dram_tensor(nm, [128, T], F32, kind="ExternalInput").ap()
              for nm in ("cosq", "sinq", "cosk", "sink")}
    q_out = nc.dram_tensor("q_out", [NH, 128, T], BF16, kind="ExternalOutput").ap()
    k_out = nc.dram_tensor("k_out", [NH, 128, T], BF16, kind="ExternalOutput").ap()
    v_out = nc.dram_tensor("v_out", [NH, T, 128], BF16, kind="ExternalOutput").ap()

    with ExitStack() as ctx:
        E = ctx.enter_context
        BLK = BlkMgr(nc)
        bufs = alloc_common(nc, E, T, with_h=False)
        psA, psB = bufs["psA"], bufs["psB"]
        xn = bufs["xn"]
        wcol = E(nc.sbuf_tensor("wcol", [128, 2048], F32))
        wcol_bf = [E(nc.sbuf_tensor(f"wcolb{i}", [128, 2048], BF16)) for i in range(2)]
        bqk_sb = E(nc.sbuf_tensor("bqk_sb", [128, 32], F32))
        bvb = E(nc.sbuf_tensor("bvb", [128, D], BF16))
        tabs = {nm: E(nc.sbuf_tensor(f"t_{nm}", [128, T], F32)) for nm in tab_in}
        qc = E(nc.sbuf_tensor("qc", [128, 512], F32))
        t2 = E(nc.sbuf_tensor("t2", [128, 512], F32))
        qb = E(nc.sbuf_tensor("qb", [128, 512], F32))
        qk_bf = [E(nc.sbuf_tensor(f"qk_bf{i}", [128, 512], BF16)) for i in range(2)]
        v_bf = [E(nc.sbuf_tensor(f"v_bf{i}", [128, 256], BF16)) for i in range(2)]
        wvb = E(nc.sbuf_tensor("wvb", [128, 4096], BF16))
        s_ld, s_ld2, s_pe, s_dv, s_sc, s_st = _mk(
            nc, ctx, ["s_ld", "s_ld2", "s_pe", "s_dv", "s_sc", "s_st"])

        blk = BLK.new("load")

        @blk.sync
        def _(sync):
            for c in range(16):
                sync.dma_start(out=bufs["x"][c][:], in_=x_in[c * 128:(c + 1) * 128, :]).then_inc(s_ld.h, 16)
            sync.dma_start(out=bqk_sb[:], in_=bqk).then_inc(s_ld.h, 16)
            for nm in tabs:
                sync.dma_start(out=tabs[nm][:], in_=tab_in[nm]).then_inc(s_ld.h, 16)
        s_ld.inc(16 * 21)

        @blk.gpsimd
        def _(gpsimd):
            gpsimd.dma_start(out=bvb[:], in_=bv.broadcast_to([128, D])).then_inc(s_ld2.h, 16)
        s_ld2.inc(16)
        x_loaded = s_ld.v

        blk = BLK.new("init")

        @blk.vector
        def _(vector):
            vector.wait_ge(s_ld.h, x_loaded)
            vector.wait_ge(s_ld2.h, 16)
            vector.memset(bufs["ones_f32"][:], 1.0)
            vector.memset(bufs["ones_bf"][:], 1.0)
            vector.memset(bufs["cst"][0:1, 0:1], LN_EPS)
            vector.memset(bufs["cst"][0:1, 1:2], RMS_EPS)
            vector.memset(bufs["cst"][0:1, 2:3], float(np.log(0.5))).then_inc(s_dv.h, 1)
        s_dv.inc(1)

        for tt in range((T + 511) // 512):
            emit_ln(nc, BLK, bufs, (s_pe, s_dv, s_sc), tt * 512, min(512, T - tt * 512), f"q{tt}")

        for j in range(32):
            p = j % 2
            h_idx = j // 2
            is_q = (j % 2) == 0
            blk = BLK.new(f"qk{j}")

            @blk.sync
            def _(sync, j=j, dv_now=s_dv.v):
                if j >= 1:
                    sync.wait_ge(s_dv.h, dv_now)
                sync.dma_start(out=wcol[:], in_=wqk[j]).then_inc(s_ld.h, 16)
            s_ld.inc(16)
            w_loaded = s_ld.v

            @blk.vector
            def _(vector, p=p, w_loaded=w_loaded, pe_now=s_pe.v):
                vector.wait_ge(s_ld.h, w_loaded)
                vector.wait_ge(s_pe.h, pe_now)
                vector.tensor_copy(out=wcol_bf[p][:], in_=wcol[:]).then_inc(s_dv.h, 1)
            s_dv.inc(1)
            w_cast = s_dv.v

            for tt in range((T + 511) // 512):
                t0 = tt * 512
                tw = min(512, T - t0)
                ps = psA if tt % 2 == 0 else psB
                op = (j * 2 + tt) % 2
                blk = BLK.new(f"qk{j}_{tt}")

                @blk.tensor
                def _(tensor, p=p, ps=ps, t0=t0, tw=tw, w_cast=w_cast, dv_now=s_dv.v):
                    tensor.wait_ge(s_dv.h, w_cast)
                    tensor.wait_ge(s_dv.h, dv_now)
                    for g in range(16):
                        ins = tensor.matmul(out=ps[:, 0:tw], lhsT=wcol_bf[p][:, g * 128:(g + 1) * 128],
                                            rhs=xn[g][:, t0:t0 + tw], start=(g == 0), stop=(g == 15))
                        if g == 15:
                            ins.then_inc(s_pe.h, 1)
                s_pe.inc(1)
                mm_done = s_pe.v

                @blk.scalar
                def _(scalar, j=j, ps=ps, tw=tw, mm_done=mm_done, dv_now=s_dv.v):
                    scalar.wait_ge(s_pe.h, mm_done)
                    scalar.wait_ge(s_dv.h, dv_now)
                    scalar.activation(out=qb[:, 0:tw], in_=ps[:, 0:tw], func=AF.Identity,
                                      bias=bqk_sb[:, j:j + 1]).then_inc(s_sc.h, 1)
                s_sc.inc(1)
                qb_done = s_sc.v

                @blk.vector
                def _(vector, is_q=is_q, t0=t0, tw=tw, qb_done=qb_done, op=op, st_now=s_st.v):
                    vector.wait_ge(s_sc.h, qb_done)
                    if st_now >= 16:
                        vector.wait_ge(s_st.h, st_now - 16 + 16)
                    cos_t = tabs["cosq"] if is_q else tabs["cosk"]
                    sin_t = tabs["sinq"] if is_q else tabs["sink"]
                    chain(vector, s_dv, [
                        lambda: vector.tensor_copy(out=t2[0:64, 0:tw], in_=qb[64:128, 0:tw]),
                        lambda: vector.tensor_copy(out=t2[64:128, 0:tw], in_=qb[0:64, 0:tw]),
                        lambda: vector.tensor_tensor(out=qc[:, 0:tw], in0=qb[:, 0:tw],
                                                     in1=cos_t[:, t0:t0 + tw], op=ALU.mult),
                        lambda: vector.tensor_tensor(out=t2[:, 0:tw], in0=t2[:, 0:tw],
                                                     in1=sin_t[:, t0:t0 + tw], op=ALU.mult),
                        lambda: vector.tensor_tensor(out=qk_bf[op][:, 0:tw], in0=qc[:, 0:tw],
                                                     in1=t2[:, 0:tw], op=ALU.add),
                    ])
                rope_done = s_dv.v

                @blk.gpsimd
                def _(gpsimd, h_idx=h_idx, is_q=is_q, t0=t0, tw=tw, rope_done=rope_done, op=op):
                    gpsimd.wait_ge(s_dv.h, rope_done)
                    dst = q_out if is_q else k_out
                    gpsimd.dma_start(out=dst[h_idx, :, t0:t0 + tw], in_=qk_bf[op][:, 0:tw]).then_inc(s_st.h, 16)
                s_st.inc(16)

        for fc in range(8):
            blk = BLK.new(f"v{fc}_ld")

            for hf in range(2):
                @blk.sync
                def _(sync, fc=fc, hf=hf, dv_now=s_dv.v):
                    sync.wait_ge(s_dv.h, dv_now)
                    sync.dma_start(out=wcol[:], in_=wv[fc, :, hf * 2048:(hf + 1) * 2048]).then_inc(s_ld.h, 16)
                s_ld.inc(16)
                wv_loaded = s_ld.v

                @blk.vector
                def _(vector, hf=hf, wv_loaded=wv_loaded, pe_now=s_pe.v):
                    vector.wait_ge(s_ld.h, wv_loaded)
                    vector.wait_ge(s_pe.h, pe_now)
                    vector.tensor_copy(out=wvb[:, hf * 2048:(hf + 1) * 2048],
                                       in_=wcol[:]).then_inc(s_dv.h, 1)
                s_dv.inc(1)
            v_cast = s_dv.v

            for t in range(T // 128):
                ps = psA if t % 2 == 0 else psB
                op = t % 2
                blk = BLK.new(f"v{fc}_{t}")

                @blk.tensor
                def _(tensor, fc=fc, t=t, ps=ps, v_cast=v_cast, dv_now=s_dv.v):
                    tensor.wait_ge(s_dv.h, v_cast)
                    tensor.wait_ge(s_dv.h, dv_now)
                    for g in range(16):
                        ins = tensor.matmul(out=ps[:, 0:256],
                                            lhsT=xn[g][:, t * 128:(t + 1) * 128],
                                            rhs=wvb[:, g * 256:(g + 1) * 256],
                                            start=(g == 0), stop=(g == 15))
                        if g == 15:
                            ins.then_inc(s_pe.h, 1)
                s_pe.inc(1)
                vmm_done = s_pe.v

                @blk.vector
                def _(vector, fc=fc, ps=ps, vmm_done=vmm_done, op=op, st_now=s_st.v):
                    vector.wait_ge(s_pe.h, vmm_done)
                    if st_now >= 32:
                        vector.wait_ge(s_st.h, st_now - 32 + 32)
                    vector.tensor_tensor(out=v_bf[op][:, 0:256], in0=ps[:, 0:256],
                                         in1=bvb[:, fc * 256:(fc + 1) * 256], op=ALU.add).then_inc(s_dv.h, 1)
                s_dv.inc(1)
                vevac = s_dv.v

                @blk.gpsimd
                def _(gpsimd, fc=fc, t=t, vevac=vevac, op=op):
                    gpsimd.wait_ge(s_dv.h, vevac)
                    for s in range(2):
                        h_i = fc * 2 + s
                        gpsimd.dma_start(out=v_out[h_i, t * 128:(t + 1) * 128, :],
                                         in_=v_bf[op][:, s * 128:(s + 1) * 128]).then_inc(s_st.h, 16)
                s_st.inc(32)

        blk = BLK.new("fin")
        final = s_st.v

        @blk.gpsimd
        def _(gpsimd):
            gpsimd.wait_ge(s_st.h, final)
        BLK.close()
    return nc



def build_attn(nbh):
    nc = bass.Bass("TRN2", target_bir_lowering=False)
    q_in = nc.dram_tensor("q_in", [nbh, 128, SEQ], BF16, kind="ExternalInput").ap()
    k_in = nc.dram_tensor("k_in", [nbh, 128, SEQ], BF16, kind="ExternalInput").ap()
    v_in = nc.dram_tensor("v_in", [nbh, 128, SEQ], BF16, kind="ExternalInput").ap()
    mask_in = nc.dram_tensor("mask_in", [128, 128], BF16, kind="ExternalInput").ap()
    a_out = nc.dram_tensor("a_out", [nbh, 128, SEQ], F32, kind="ExternalOutput").ap()

    NQ = SEQ // 512

    with ExitStack() as ctx:
        E = ctx.enter_context
        BLK = BlkMgr(nc)
        q_sb = E(nc.sbuf_tensor("q_sb", [128, SEQ], BF16))
        k_sb = E(nc.sbuf_tensor("k_sb", [128, SEQ], BF16))
        v_sb = E(nc.sbuf_tensor("v_sb", [128, SEQ], BF16))
        msk = E(nc.sbuf_tensor("msk", [128, 128], BF16))
        pt = [E(nc.sbuf_tensor(f"pt{i}", [128, 512], BF16)) for i in range(2)]
        ones_bf = E(nc.sbuf_tensor("ones_bf", [128, 1], BF16))
        ones_f32 = E(nc.sbuf_tensor("ones_f32", [1, 128], F32))
        r_sb = E(nc.sbuf_tensor("r_sb", [1, 512], F32))
        rb_sb = E(nc.sbuf_tensor("rb_sb", [128, 512], F32))
        o_sb = [E(nc.sbuf_tensor(f"o_sb{i}", [128, 512], F32)) for i in range(2)]
        ps_st = [E(nc.psum_tensor(f"ps_st{i}", [128, 512], F32)) for i in range(2)]
        ps_pv = E(nc.psum_tensor("ps_pv", [128, 512], F32))
        ps_l = E(nc.psum_tensor("ps_l", [1, 512], F32))
        ps_rb = E(nc.psum_tensor("ps_rb", [128, 512], F32))
        s_ld, s_pe, s_dv, s_sc, s_st = _mk(nc, ctx, ["s_ld", "s_pe", "s_dv", "s_sc", "s_st"])

        blk = BLK.new("init")

        @blk.vector
        def _(vector):
            vector.memset(ones_bf[:], 1.0)
            vector.memset(ones_f32[:], 1.0).then_inc(s_dv.h, 1)
        s_dv.inc(1)
        init_done = s_dv.v

        @blk.sync
        def _(sync):
            sync.dma_start(out=msk[:], in_=mask_in).then_inc(s_ld.h, 16)
        s_ld.inc(16)

        last_exp = {0: 0, 1: 0}
        last_lpv = {0: 0, 1: 0}
        prev_norm_dv = 0

        for bh in range(nbh):
            blk = BLK.new(f"bh{bh}_ld")

            @blk.sync
            def _(sync, bh=bh, st_now=s_st.v):
                if bh > 0:
                    sync.wait_ge(s_st.h, st_now)
                sync.dma_start(out=q_sb[:], in_=q_in[bh]).then_inc(s_ld.h, 16)
                sync.dma_start(out=k_sb[:], in_=k_in[bh]).then_inc(s_ld.h, 16)
                sync.dma_start(out=v_sb[:], in_=v_in[bh]).then_inc(s_ld.h, 16)
            s_ld.inc(48)
            kv_loaded = s_ld.v

            for qb in range(NQ):
                n_kt = 4 * qb + 4
                for kt in range(n_kt):
                    par = kt % 2
                    scol0 = max(0, (kt - qb * 4)) * 128
                    diag = kt >= qb * 4
                    blk = BLK.new(f"b{bh}q{qb}k{kt}")

                    @blk.tensor
                    def _(tensor, qb=qb, kt=kt, par=par, scol0=scol0,
                          kv_loaded=kv_loaded, le=last_exp[par], first=(qb == 0 and kt == 0)):
                        if first:
                            tensor.wait_ge(s_ld.h, kv_loaded)
                            tensor.wait_ge(s_dv.h, init_done)
                        if le > 0:
                            tensor.wait_ge(s_sc.h, le)
                        tensor.matmul(out=ps_st[par][:, scol0:512],
                                      lhsT=k_sb[:, kt * 128:(kt + 1) * 128],
                                      rhs=q_sb[:, qb * 512 + scol0:qb * 512 + 512],
                                      start=True, stop=True).then_inc(s_pe.h, 1)
                    s_pe.inc(1)
                    st_done = s_pe.v

                    @blk.scalar
                    def _(scalar, par=par, scol0=scol0, st_done=st_done, lp=last_lpv[par]):
                        scalar.wait_ge(s_pe.h, st_done)
                        if lp > 0:
                            scalar.wait_ge(s_pe.h, lp)
                        scalar.activation(out=pt[par][:, scol0:512], in_=ps_st[par][:, scol0:512],
                                          func=AF.Exp).then_inc(s_sc.h, 1)
                    s_sc.inc(1)
                    last_exp[par] = s_sc.v
                    exp_done = s_sc.v

                    if diag:
                        @blk.vector
                        def _(vector, par=par, scol0=scol0, exp_done=exp_done):
                            vector.wait_ge(s_sc.h, exp_done)
                            vector.tensor_tensor(out=pt[par][:, scol0:scol0 + 128],
                                                 in0=pt[par][:, scol0:scol0 + 128],
                                                 in1=msk[:], op=ALU.mult).then_inc(s_dv.h, 1)
                        s_dv.inc(1)
                        rsem, rval = s_dv, s_dv.v
                    else:
                        rsem, rval = s_sc, exp_done

                    @blk.tensor
                    def _(tensor, par=par, scol0=scol0, kt=kt, n_kt=n_kt, rsem=rsem, rval=rval,
                          pnorm=prev_norm_dv):
                        tensor.wait_ge(rsem.h, rval)
                        if kt == 0 and pnorm > 0:
                            tensor.wait_ge(s_dv.h, pnorm)
                        tensor.matmul(out=ps_l[0:1, scol0:512], lhsT=ones_bf[:, 0:1],
                                      rhs=pt[par][:, scol0:512], start=(kt == 0), stop=(kt == n_kt - 1))
                        tensor.matmul(out=ps_pv[:, scol0:512], lhsT=v_sb[:, kt * 128:(kt + 1) * 128],
                                      rhs=pt[par][:, scol0:512], start=(kt == 0),
                                      stop=(kt == n_kt - 1)).then_inc(s_pe.h, 1)
                    s_pe.inc(1)
                    last_lpv[par] = s_pe.v

                pv_done = s_pe.v
                op = qb % 2
                blk = BLK.new(f"b{bh}q{qb}n")

                @blk.vector
                def _(vector, pv_done=pv_done):
                    vector.wait_ge(s_pe.h, pv_done)
                    vector.reciprocal(out=r_sb[:], in_=ps_l[0:1, :]).then_inc(s_dv.h, 1)
                s_dv.inc(1)
                r_done = s_dv.v

                @blk.tensor
                def _(tensor, r_done=r_done):
                    tensor.wait_ge(s_dv.h, r_done)
                    tensor.matmul(out=ps_rb[:], lhsT=ones_f32[:], rhs=r_sb[:],
                                  start=True, stop=True).then_inc(s_pe.h, 1)
                s_pe.inc(1)
                rb_done = s_pe.v

                @blk.vector
                def _(vector, op=op, rb_done=rb_done, st_now=s_st.v):
                    vector.wait_ge(s_pe.h, rb_done)
                    if st_now >= 32:
                        vector.wait_ge(s_st.h, st_now - 32 + 32)
                    chain(vector, s_dv, [
                        lambda: vector.tensor_copy(out=rb_sb[:], in_=ps_rb[:]),
                        lambda: vector.tensor_tensor(out=o_sb[op][:], in0=ps_pv[:],
                                                     in1=rb_sb[:], op=ALU.mult),
                    ])
                o_done = s_dv.v
                prev_norm_dv = o_done

                @blk.gpsimd
                def _(gpsimd, bh=bh, qb=qb, op=op, o_done=o_done):
                    gpsimd.wait_ge(s_dv.h, o_done)
                    gpsimd.dma_start(out=a_out[bh, :, qb * 512:(qb + 1) * 512],
                                     in_=o_sb[op][:]).then_inc(s_st.h, 16)
                s_st.inc(16)

        blk = BLK.new("fin")
        final = s_st.v

        @blk.gpsimd
        def _(gpsimd):
            gpsimd.wait_ge(s_st.h, final)
        BLK.close()
    return nc



def _rope_tables():
    inv = 1.0 / (10000.0 ** (np.arange(0, HD, 2, dtype=np.float64) / HD))
    t = np.arange(SEQ, dtype=np.float64)
    fr = np.outer(t, inv)
    emb = np.concatenate([fr, fr], axis=1)
    cos = np.cos(emb).astype(np.float32).T
    sin = np.sin(emb).astype(np.float32).T
    sinA = sin.copy()
    sinA[0:64] = -sin[0:64]
    return cos, sinA


def kernel(L0_attn_hidden_states, L0_attn_w_qkv, L0_attn_b_qkv,
           L0_ffn_gate_up_packed, L0_ffn_gate_up_scales,
           L0_ffn_down_packed, L0_ffn_down_scales):
    x = np.asarray(L0_attn_hidden_states)
    w_qkv = np.asarray(L0_attn_w_qkv)
    b_qkv = np.asarray(L0_attn_b_qkv)
    gup = np.asarray(L0_ffn_gate_up_packed)
    gus = np.asarray(L0_ffn_gate_up_scales)
    dnp = np.asarray(L0_ffn_down_packed)
    dns = np.asarray(L0_ffn_down_scales)

    cores = list(range(NC))

    gw, dw = 2 * NI // NC, D // NC
    gb = np.ascontiguousarray(np.repeat(gup.view(np.uint8).reshape(2 * NI, 1024).T, 2, axis=0))
    db = np.ascontiguousarray(np.repeat(dnp.view(np.uint8).reshape(D, 2816).T, 2, axis=0))
    shift = (4 * (np.arange(128, dtype=np.uint8) % 2)).reshape(128, 1)
    nc1 = build_dequant(gw, dw)
    maps1 = [{
        "b2g": np.ascontiguousarray(gb[:, c * gw:(c + 1) * gw]),
        "sg": np.ascontiguousarray(gus[:, c * gw:(c + 1) * gw]),
        "b2d": np.ascontiguousarray(db[:, c * dw:(c + 1) * dw]),
        "sd": np.ascontiguousarray(dns[:, c * dw:(c + 1) * dw]),
        "shift": shift,
    } for c in cores]
    r1 = run_bass_kernel_spmd(nc1, maps1, core_ids=cores).results
    wguT = np.concatenate([r1[c]["wg"] for c in cores], axis=1)
    wdnT = np.concatenate([r1[c]["wd"] for c in cores], axis=1)
    wgu_cols = np.ascontiguousarray(
        wguT.reshape(16, 128, 88, 128).transpose(2, 1, 0, 3).reshape(88, 128, 2048))
    wdn_cols = np.ascontiguousarray(
        wdnT.reshape(44, 128, 16, 128).transpose(2, 1, 0, 3).reshape(16, 128, NI))

    T = SEQ * NB // NC
    xT = np.ascontiguousarray(x.transpose(0, 2, 1))
    shards = [(c // 4, (c % 4) * T) for c in cores]

    nc2 = build_ffn(T, with_attn_add=False)
    maps2 = [{"x_in": np.ascontiguousarray(xT[b][:, t0:t0 + T]), "wgu": wgu_cols, "wdn": wdn_cols}
             for (b, t0) in shards]
    r2 = run_bass_kernel_spmd(nc2, maps2, core_ids=cores).results
    x1T = np.empty((NB, D, SEQ), np.float32)
    for c, (b, t0) in enumerate(shards):
        x1T[b][:, t0:t0 + T] = r2[c]["x_out"]

    cos, sinA = _rope_tables()
    wr = w_qkv.reshape(NH, 3, HD, D)
    wqkT = wr[:, :2].transpose(3, 0, 1, 2).reshape(D, 2 * D)
    wqk_cols = np.ascontiguousarray(
        wqkT.reshape(16, 128, 32, 128).transpose(2, 1, 0, 3).reshape(32, 128, 2048))
    wvT = wr[:, 2].transpose(2, 0, 1).reshape(D, D)
    wv_cols = np.ascontiguousarray(
        wvT.reshape(16, 128, 8, 256).transpose(2, 1, 0, 3).reshape(8, 128, 4096))
    br = b_qkv.reshape(NH, 3, HD)
    bqk_t = np.ascontiguousarray(br[:, :2].reshape(NH * 2, HD).T)
    bv_row = np.ascontiguousarray(br[:, 2].reshape(1, D))
    nc3 = build_qkv(T)
    maps3 = [{
        "x_in": np.ascontiguousarray(x1T[b][:, t0:t0 + T]),
        "wqk": wqk_cols, "wv": wv_cols, "bqk": bqk_t, "bv": bv_row,
        "cosq": np.ascontiguousarray(cos[:, t0:t0 + T] * np.float32(SM_SCALE)),
        "sinq": np.ascontiguousarray(sinA[:, t0:t0 + T] * np.float32(SM_SCALE)),
        "cosk": np.ascontiguousarray(cos[:, t0:t0 + T]),
        "sink": np.ascontiguousarray(sinA[:, t0:t0 + T]),
    } for (b, t0) in shards]
    r3 = run_bass_kernel_spmd(nc3, maps3, core_ids=cores).results
    qf = np.empty((NB, NH, HD, SEQ), BF)
    kf = np.empty((NB, NH, HD, SEQ), BF)
    vf = np.empty((NB, NH, SEQ, HD), BF)
    for c, (b, t0) in enumerate(shards):
        qf[b][:, :, t0:t0 + T] = r3[c]["q_out"]
        kf[b][:, :, t0:t0 + T] = r3[c]["k_out"]
        vf[b][:, t0:t0 + T, :] = r3[c]["v_out"]

    nbh = NB * NH // NC
    maskT = np.triu(np.ones((128, 128), np.float32)).astype(BF)
    nc4 = build_attn(nbh)
    maps4 = []
    for c in cores:
        idx = [(g // NH, g % NH) for g in range(c * nbh, (c + 1) * nbh)]
        maps4.append({
            "q_in": np.ascontiguousarray(np.stack([np.asarray(qf[b][h]) for (b, h) in idx])),
            "k_in": np.ascontiguousarray(np.stack([np.asarray(kf[b][h]) for (b, h) in idx])),
            "v_in": np.ascontiguousarray(np.stack(
                [np.asarray(vf[b][h]).reshape(32, 128, 128).transpose(1, 0, 2).reshape(128, SEQ)
                 for (b, h) in idx])),
            "mask_in": maskT,
        })
    r4 = run_bass_kernel_spmd(nc4, maps4, core_ids=cores).results
    attnT = np.empty((NB, D, SEQ), np.float32)
    for c in cores:
        idx = [(g // NH, g % NH) for g in range(c * nbh, (c + 1) * nbh)]
        for i, (b, h) in enumerate(idx):
            attnT[b][h * HD:(h + 1) * HD, :] = r4[c]["a_out"][i]

    nc5 = build_ffn(T, with_attn_add=True)
    maps5 = [{"x_in": np.ascontiguousarray(x1T[b][:, t0:t0 + T]),
              "attn_in": np.ascontiguousarray(attnT[b][:, t0:t0 + T]),
              "wgu": wgu_cols, "wdn": wdn_cols}
             for (b, t0) in shards]
    r5 = run_bass_kernel_spmd(nc5, maps5, core_ids=cores).results
    outT = np.empty((NB, D, SEQ), np.float32)
    for c, (b, t0) in enumerate(shards):
        outT[b][:, t0:t0 + T] = r5[c]["x_out"]
    return np.ascontiguousarray(outT.transpose(0, 2, 1))


# version 04
# speedup vs baseline: 1.0075x; 1.0075x over previous
import numpy as np
import ml_dtypes
from contextlib import ExitStack

import concourse.bass as bass
import concourse.mybir as mybir
from concourse.bass_utils import run_bass_kernel_spmd

F32 = mybir.dt.float32
BF16 = mybir.dt.bfloat16
F16 = mybir.dt.float16
I32 = mybir.dt.int32
U16 = mybir.dt.uint16
U8 = mybir.dt.uint8
AF = mybir.ActivationFunctionType
ALU = mybir.AluOpType
BF = ml_dtypes.bfloat16

NC = 8
D = 2048
NI = 5632
NH = 16
HD = 128
SEQ = 4096
NB = 2
LN_EPS = 1e-5
RMS_EPS = 1e-6
SM_SCALE = 1.0 / np.sqrt(128.0)



class BlkMgr:

    def __init__(self, nc):
        self.nc = nc
        self.cm = None

    def new(self, name):
        if self.cm is not None:
            self.cm.__exit__(None, None, None)
        self.cm = self.nc.Block(name)
        return self.cm.__enter__()

    def close(self):
        if self.cm is not None:
            self.cm.__exit__(None, None, None)
            self.cm = None


class Sem:
    def __init__(self, handle):
        self.h = handle
        self.v = 0

    def inc(self, amt=1):
        self.v += amt
        return self.v


def _mk(nc, ctx, names):
    return [Sem(ctx.enter_context(nc.semaphore(n))) for n in names]


def chain(eng, sem, fns):
    last = None
    for k, f in enumerate(fns):
        if k > 0:
            eng.wait_ge(sem.h, sem.v)
        ins = f()
        ins.then_inc(sem.h, 1)
        sem.inc(1)
        last = ins
    return last



def emit_dequant_tiles(nc, BLK, sems, tiles, bufs, wmax):
    s_in, s_dec, s_out = sems
    byt, scb = bufs["byt"], bufs["scb"]
    nr8, n15, rb1, rb2, rm, ru, rv, bits, wt = (
        bufs["nr8"], bufs["n15"], bufs["rb1"], bufs["rb2"], bufs["rm"],
        bufs["ru"], bufs["rv"], bufs["bits"], bufs["wt"])

    for i, (g, w, b2, sc, out) in enumerate(tiles):
        p = i % 2
        blk = BLK.new(f"dq{i}")

        @blk.sync
        def _(sync, g=g, w=w, b2=b2, sc=sc, p=p, i=i, dec_prev=s_dec.v):
            if i >= 2:
                sync.wait_ge(s_dec.h, dec_prev)
            sync.dma_start(out=byt[p][:, 0:w], in_=b2[g * 128:(g + 1) * 128, :]).then_inc(s_in.h, 16)
            sync.dma_start(out=scb[p][:, 0:w],
                           in_=sc[g:g + 1, :].broadcast_to([128, w])).then_inc(s_in.h, 16)
        s_in.inc(32)
        loaded = s_in.v

        @blk.vector
        def _(vector, g=g, w=w, p=p, i=i, loaded=loaded, st_prev=s_out.v):
            vector.wait_ge(s_in.h, loaded)
            if i >= 2:
                vector.wait_ge(s_out.h, st_prev - 16 + 16)
            B = byt[p][:, 0:w]
            chain(vector, s_dec, [
                lambda: vector.tensor_scalar(out=nr8[0:64, 0:w], in0=B[0:64], scalar1=15,
                                             scalar2=None, op0=ALU.bitwise_and),
                lambda: vector.tensor_scalar(out=nr8[64:128, 0:w], in0=B[64:128], scalar1=4,
                                             scalar2=None, op0=ALU.logical_shift_right),
                lambda: vector.tensor_copy(out=n15[:, 0:w], in_=nr8[:, 0:w]),
                lambda: vector.tensor_scalar(out=rb1[:, 0:w], in0=n15[:, 0:w], scalar1=8, scalar2=12,
                                             op0=ALU.bitwise_and, op1=ALU.logical_shift_left),
                lambda: vector.tensor_scalar(out=rb2[:, 0:w], in0=n15[:, 0:w], scalar1=7, scalar2=6,
                                             op0=ALU.bitwise_and, op1=ALU.logical_shift_left),
                lambda: vector.tensor_scalar(out=rm[:, 0:w], in0=n15[:, 0:w], scalar1=6,
                                             scalar2=None, op0=ALU.bitwise_and),
                lambda: vector.tensor_scalar(out=ru[:, 0:w], in0=rm[:, 0:w], scalar1=0, scalar2=16128,
                                             op0=ALU.is_equal, op1=ALU.mult),
                lambda: vector.tensor_scalar(out=rm[:, 0:w], in0=n15[:, 0:w], scalar1=7,
                                             scalar2=None, op0=ALU.bitwise_and),
                lambda: vector.tensor_scalar(out=rv[:, 0:w], in0=rm[:, 0:w], scalar1=1, scalar2=16064,
                                             op0=ALU.is_equal, op1=ALU.mult),
                lambda: vector.scalar_tensor_tensor(out=rb1[:, 0:w], in0=rb1[:, 0:w], scalar=16128.0,
                                                    in1=rb2[:, 0:w], op0=ALU.add, op1=ALU.add),
                lambda: vector.tensor_tensor(out=rb1[:, 0:w], in0=rb1[:, 0:w], in1=ru[:, 0:w],
                                             op=ALU.subtract),
                lambda: vector.tensor_tensor(out=bits[:, 0:w], in0=rb1[:, 0:w], in1=rv[:, 0:w],
                                             op=ALU.add),
                lambda: vector.tensor_tensor(out=wt[p][:, 0:w], in0=bits[:, 0:w].bitcast(BF16),
                                             in1=scb[p][:, 0:w], op=ALU.mult),
            ])
        dec = s_dec.v

        @blk.gpsimd
        def _(gpsimd, g=g, w=w, out=out, p=p, dec=dec):
            gpsimd.wait_ge(s_dec.h, dec)
            gpsimd.dma_start(out=out[g * 128:(g + 1) * 128, :], in_=wt[p][:, 0:w]).then_inc(s_out.h, 16)
        s_out.inc(16)


def alloc_dequant_bufs(nc, E, wmax):
    bufs = {}
    bufs["byt"] = [E(nc.sbuf_tensor(f"byt{i}", [128, wmax], U8)) for i in range(2)]
    bufs["scb"] = [E(nc.sbuf_tensor(f"scb{i}", [128, wmax], F16)) for i in range(2)]
    bufs["nr8"] = E(nc.sbuf_tensor("nr8", [128, wmax], U8))
    for nm in ("n15", "rb1", "rb2", "rm", "ru", "rv"):
        bufs[nm] = E(nc.sbuf_tensor(nm, [128, wmax], I32))
    bufs["bits"] = E(nc.sbuf_tensor("bits", [128, wmax], U16))
    bufs["wt"] = [E(nc.sbuf_tensor(f"wtile{i}", [128, wmax], BF16)) for i in range(2)]
    return bufs


def build_dequant(gw, dw):
    nc = bass.Bass("TRN2", target_bir_lowering=False)
    b2g = nc.dram_tensor("b2g", [D, gw], U8, kind="ExternalInput").ap()
    sg = nc.dram_tensor("sg", [16, gw], F16, kind="ExternalInput").ap()
    b2d = nc.dram_tensor("b2d", [NI, dw], U8, kind="ExternalInput").ap()
    sd = nc.dram_tensor("sd", [44, dw], F16, kind="ExternalInput").ap()
    wg_out = nc.dram_tensor("wg", [D, gw], BF16, kind="ExternalOutput").ap()
    wd_out = nc.dram_tensor("wd", [NI, dw], BF16, kind="ExternalOutput").ap()

    wmax = max(gw, dw)
    with ExitStack() as ctx:
        E = ctx.enter_context
        BLK = BlkMgr(nc)
        bufs = alloc_dequant_bufs(nc, E, wmax)
        s_in, s_dec, s_out = _mk(nc, ctx, ["s_in", "s_dec", "s_out"])

        tiles = [(g, gw, b2g, sg, wg_out) for g in range(16)]
        tiles += [(g, dw, b2d, sd, wd_out) for g in range(44)]
        emit_dequant_tiles(nc, BLK, (s_in, s_dec, s_out), tiles, bufs, wmax)

        blk = BLK.new("fin")
        final = s_out.v

        @blk.gpsimd
        def _(gpsimd):
            gpsimd.wait_ge(s_out.h, final)
        BLK.close()
    return nc



def emit_ln(nc, BLK, bufs, sems, t0, tw, tag):
    s_pe, s_dv, s_sc = sems
    x, xn = bufs["x"], bufs["xn"]
    xbf, sqb = bufs["xbf"], bufs["sqb"]
    rows = bufs["rows"]
    a_b, b_b, tmp = bufs["a_b"], bufs["b_b"], bufs["tmp"]
    psS1, psS2, psC, psD = bufs["psS1"], bufs["psS2"], bufs["psC"], bufs["psD"]
    ones_bf, ones_f32 = bufs["ones_bf"], bufs["ones_f32"]

    blk = BLK.new(f"ln_{tag}")

    @blk.vector
    def _(vector, dv_now=s_dv.v):
        vector.wait_ge(s_dv.h, dv_now)
        for c in range(16):
            ins = vector.tensor_copy(out=xbf[c][:, 0:tw], in_=x[c][:, t0:t0 + tw])
            if c == 15:
                ins.then_inc(s_dv.h, 1)
    s_dv.inc(1)
    cast_done = s_dv.v

    @blk.scalar
    def _(scalar):
        scalar.wait_ge(s_dv.h, cast_done)
        for c in range(16):
            ins = scalar.activation(out=sqb[c][:, 0:tw], in_=xbf[c][:, 0:tw], func=AF.Square)
            if c == 15:
                ins.then_inc(s_sc.h, 1)
    s_sc.inc(1)
    sq_done = s_sc.v

    @blk.tensor
    def _(tensor):
        tensor.wait_ge(s_sc.h, sq_done)
        for c in range(16):
            tensor.matmul(out=psS1[0:1, 0:tw], lhsT=ones_bf[:, 0:1], rhs=xbf[c][:, 0:tw],
                          start=(c == 0), stop=(c == 15))
        for c in range(16):
            ins = tensor.matmul(out=psS2[0:1, 0:tw], lhsT=ones_bf[:, 0:1], rhs=sqb[c][:, 0:tw],
                                start=(c == 0), stop=(c == 15))
            if c == 15:
                ins.then_inc(s_pe.h, 1)
    s_pe.inc(1)
    stats_done = s_pe.v

    blk = BLK.new(f"ln_{tag}_r")

    @blk.vector
    def _(vector):
        vector.wait_ge(s_pe.h, stats_done)
        chain(vector, s_dv, [
            lambda: vector.tensor_scalar(out=rows[0][:, 0:tw], in0=psS1[0:1, 0:tw],
                                         scalar1=1.0 / D, scalar2=None, op0=ALU.mult),
            lambda: vector.tensor_scalar(out=rows[1][:, 0:tw], in0=psS2[0:1, 0:tw],
                                         scalar1=1.0 / D, scalar2=None, op0=ALU.mult),
            lambda: vector.tensor_tensor(out=rows[2][:, 0:tw], in0=rows[0][:, 0:tw],
                                         in1=rows[0][:, 0:tw], op=ALU.mult),
            lambda: vector.tensor_tensor(out=rows[2][:, 0:tw], in0=rows[1][:, 0:tw],
                                         in1=rows[2][:, 0:tw], op=ALU.subtract),
        ])
    var_done = s_dv.v

    @blk.scalar
    def _(scalar):
        scalar.wait_ge(s_dv.h, var_done)
        chain(scalar, s_sc, [
            lambda: scalar.activation(out=rows[3][:, 0:tw], in_=rows[2][:, 0:tw], func=AF.Ln,
                                      bias=bufs["cst"][0:1, 0:1]),
            lambda: scalar.activation(out=rows[4][:, 0:tw], in_=rows[3][:, 0:tw], func=AF.Exp,
                                      scale=-0.5),
        ])
    rstd_done = s_sc.v

    @blk.vector
    def _(vector):
        vector.wait_ge(s_sc.h, rstd_done)
        vector.wait_ge(s_dv.h, s_dv.v)
        vector.tensor_tensor(out=rows[5][:, 0:tw], in0=rows[0][:, 0:tw], in1=rows[4][:, 0:tw],
                             op=ALU.mult).then_inc(s_dv.h, 1)
    s_dv.inc(1)
    mr_done = s_dv.v

    @blk.tensor
    def _(tensor):
        tensor.wait_ge(s_dv.h, mr_done)
        tensor.matmul(out=psC[:, 0:tw], lhsT=ones_f32[0:1, :], rhs=rows[4][:, 0:tw],
                      start=True, stop=True)
        tensor.matmul(out=psD[:, 0:tw], lhsT=ones_f32[0:1, :], rhs=rows[5][:, 0:tw],
                      start=True, stop=True).then_inc(s_pe.h, 1)
    s_pe.inc(1)
    bc_done = s_pe.v

    blk = BLK.new(f"ln_{tag}_a")

    @blk.vector
    def _(vector):
        vector.wait_ge(s_pe.h, bc_done)
        fns = [lambda: vector.tensor_copy(out=a_b[:, 0:tw], in_=psC[:, 0:tw]),
               lambda: vector.tensor_copy(out=b_b[:, 0:tw], in_=psD[:, 0:tw])]
        for c in range(16):
            fns.append(lambda c=c: vector.tensor_tensor(
                out=tmp[:, 0:tw], in0=x[c][:, t0:t0 + tw], in1=a_b[:, 0:tw], op=ALU.mult))
            fns.append(lambda c=c: vector.tensor_tensor(
                out=xn[c][:, t0:t0 + tw], in0=tmp[:, 0:tw], in1=b_b[:, 0:tw], op=ALU.subtract))
        chain(vector, s_dv, fns)
    return s_dv.v


def alloc_common(nc, E, T, with_h):
    bufs = {}
    bufs["x"] = [E(nc.sbuf_tensor(f"x{c}", [128, T], F32)) for c in range(16)]
    bufs["xn"] = [E(nc.sbuf_tensor(f"xn{c}", [128, T], BF16)) for c in range(16)]
    if with_h:
        bufs["h"] = [E(nc.sbuf_tensor(f"h{i}", [128, 512], BF16)) for i in range(44)]
        bufs["xbf"] = bufs["h"][0:16]
        bufs["sqb"] = bufs["h"][16:32]
        if T >= 1024:
            bufs["y"] = [bufs["xn"][c][:].bitcast(F32)[:, 0:512] for c in range(16)]
        else:
            bufs["y"] = [E(nc.sbuf_tensor(f"y{c}", [128, 512], F32))[:] for c in range(16)]
    else:
        bufs["xbf"] = [E(nc.sbuf_tensor(f"xbf{c}", [128, 512], BF16)) for c in range(16)]
        bufs["sqb"] = [E(nc.sbuf_tensor(f"sqb{c}", [128, 512], BF16)) for c in range(16)]
    bufs["rows"] = [E(nc.sbuf_tensor(f"row{i}", [1, 512], F32)) for i in range(8)]
    bufs["a_b"] = E(nc.sbuf_tensor("a_b", [128, 512], F32))
    bufs["b_b"] = E(nc.sbuf_tensor("b_b", [128, 512], F32))
    bufs["tmp"] = E(nc.sbuf_tensor("tmp", [128, 512], F32))
    bufs["ones_bf"] = E(nc.sbuf_tensor("ones_bf", [128, 2], BF16))
    bufs["ones_f32"] = E(nc.sbuf_tensor("ones_f32", [1, 128], F32))
    bufs["cst"] = E(nc.sbuf_tensor("cst", [1, 4], F32))
    bufs["psS1"] = E(nc.psum_tensor("psS1", [1, 512], F32))
    bufs["psS2"] = E(nc.psum_tensor("psS2", [1, 512], F32))
    bufs["psA"] = E(nc.psum_tensor("psA", [128, 512], F32))
    bufs["psB"] = E(nc.psum_tensor("psB", [128, 512], F32))
    bufs["psC"] = E(nc.psum_tensor("psC", [128, 512], F32))
    bufs["psD"] = E(nc.psum_tensor("psD", [128, 512], F32))
    return bufs



def emit_ffn_body(nc, BLK, bufs, sems, T, wgu, wdn, tag):
    s_ld, s_ld2, s_pe, s_dv, s_sc = sems
    x, xn = bufs["x"], bufs["xn"]
    rows = bufs["rows"]
    psS1, psA, psB, psC = bufs["psS1"], bufs["psA"], bufs["psB"], bufs["psC"]
    ones_bf, ones_f32 = bufs["ones_bf"], bufs["ones_f32"]
    wgc, wuc, wdc = bufs["wgc"], bufs["wuc"], bufs["wdc"]
    sg_bf, h, y = bufs["sg_bf"], bufs["h"], bufs["y"]
    sqy, rb, tmp = bufs["sqy"], bufs["rb"], bufs["tmp"]
    cst = bufs["cst"]

    n_tt = (T + 511) // 512
    for tt in range(n_tt):
        t0 = tt * 512
        tw = min(512, T - t0)
        xn_ready = emit_ln(nc, BLK, bufs, (s_pe, s_dv, s_sc), t0, tw, f"{tag}{tt}")

        for i in range(44):
            p = i % 2
            blk = BLK.new(f"gu_{tag}{tt}_{i}")

            @blk.sync
            def _(sync, i=i, p=p, pe_now=s_pe.v):
                if i >= 2:
                    sync.wait_ge(s_pe.h, pe_now)
                sync.dma_start(out=wgc[p][:], in_=wgu[i]).then_inc(s_ld.h, 16)
            s_ld.inc(16)
            g_loaded = s_ld.v

            @blk.gpsimd
            def _(gpsimd, i=i, p=p, pe_now=s_pe.v):
                if i >= 2:
                    gpsimd.wait_ge(s_pe.h, pe_now)
                gpsimd.dma_start(out=wuc[p][:], in_=wgu[44 + i]).then_inc(s_ld2.h, 16)
            s_ld2.inc(16)
            u_loaded = s_ld2.v

            @blk.tensor
            def _(tensor, i=i, p=p, g_loaded=g_loaded, u_loaded=u_loaded,
                  xn_ready=xn_ready, tw=tw, t0=t0, dv_now=s_dv.v, sc_now=s_sc.v):
                tensor.wait_ge(s_ld.h, g_loaded)
                tensor.wait_ge(s_ld2.h, u_loaded)
                if i == 0:
                    tensor.wait_ge(s_dv.h, xn_ready)
                else:
                    tensor.wait_ge(s_dv.h, dv_now)
                    tensor.wait_ge(s_sc.h, sc_now)
                for g in range(16):
                    tensor.matmul(out=psA[:, 0:tw], lhsT=wgc[p][:, g * 128:(g + 1) * 128],
                                  rhs=xn[g][:, t0:t0 + tw], start=(g == 0), stop=(g == 15))
                for g in range(16):
                    ins = tensor.matmul(out=psB[:, 0:tw], lhsT=wuc[p][:, g * 128:(g + 1) * 128],
                                        rhs=xn[g][:, t0:t0 + tw], start=(g == 0), stop=(g == 15))
                    if g == 15:
                        ins.then_inc(s_pe.h, 1)
            s_pe.inc(1)
            mm_done = s_pe.v

            @blk.scalar
            def _(scalar, mm_done=mm_done, tw=tw):
                scalar.wait_ge(s_pe.h, mm_done)
                scalar.activation(out=sg_bf[:, 0:tw], in_=psA[:, 0:tw], func=AF.Silu).then_inc(s_sc.h, 1)
            s_sc.inc(1)
            silu_done = s_sc.v

            @blk.vector
            def _(vector, i=i, silu_done=silu_done, tw=tw):
                vector.wait_ge(s_sc.h, silu_done)
                vector.tensor_tensor(out=h[i][:, 0:tw], in0=sg_bf[:, 0:tw], in1=psB[:, 0:tw],
                                     op=ALU.mult).then_inc(s_dv.h, 1)
            s_dv.inc(1)

        h_all_done = s_dv.v
        for m in range(16):
            p = m % 2
            ps = psA if p == 0 else psB
            blk = BLK.new(f"dn_{tag}{tt}_{m}")

            @blk.sync
            def _(sync, m=m, p=p, pe_now=s_pe.v):
                if m >= 2:
                    sync.wait_ge(s_pe.h, pe_now)
                sync.dma_start(out=wdc[p][:], in_=wdn[m]).then_inc(s_ld.h, 16)
            s_ld.inc(16)
            d_loaded = s_ld.v

            @blk.tensor
            def _(tensor, m=m, ps=ps, p=p, d_loaded=d_loaded, h_all_done=h_all_done,
                  tw=tw, dv_now=s_dv.v, sc_now=s_sc.v):
                tensor.wait_ge(s_ld.h, d_loaded)
                if m == 0:
                    tensor.wait_ge(s_dv.h, h_all_done)
                if m >= 2:
                    tensor.wait_ge(s_dv.h, dv_now)
                    tensor.wait_ge(s_sc.h, sc_now)
                for g in range(44):
                    ins = tensor.matmul(out=ps[:, 0:tw], lhsT=wdc[p][:, g * 128:(g + 1) * 128],
                                        rhs=h[g][:, 0:tw], start=(g == 0), stop=(g == 43))
                    if g == 43:
                        ins.then_inc(s_pe.h, 1)
            s_pe.inc(1)
            dmm_done = s_pe.v

            @blk.scalar
            def _(scalar, ps=ps, dmm_done=dmm_done, tw=tw):
                scalar.wait_ge(s_pe.h, dmm_done)
                scalar.activation(out=sqy[:, 0:tw], in_=ps[:, 0:tw], func=AF.Square).then_inc(s_sc.h, 1)
            s_sc.inc(1)
            sq_done = s_sc.v

            @blk.vector
            def _(vector, m=m, ps=ps, dmm_done=dmm_done, tw=tw):
                vector.wait_ge(s_pe.h, dmm_done)
                vector.tensor_copy(out=y[m][:, 0:tw], in_=ps[:, 0:tw]).then_inc(s_dv.h, 1)
            s_dv.inc(1)

            @blk.tensor
            def _(tensor, m=m, sq_done=sq_done, tw=tw):
                tensor.wait_ge(s_sc.h, sq_done)
                ins = tensor.matmul(out=psS1[0:1, 0:tw], lhsT=ones_bf[:, 0:1], rhs=sqy[:, 0:tw],
                                    start=(m == 0), stop=(m == 15))
                if m == 15:
                    ins.then_inc(s_pe.h, 1)
            if m == 15:
                s_pe.inc(1)
        ss_done = s_pe.v

        blk = BLK.new(f"rms_{tag}{tt}")

        @blk.scalar
        def _(scalar, ss_done=ss_done, tw=tw):
            scalar.wait_ge(s_pe.h, ss_done)
            chain(scalar, s_sc, [
                lambda: scalar.activation(out=rows[6][:, 0:tw], in_=psS1[0:1, 0:tw], func=AF.Ln,
                                          scale=1.0 / D, bias=cst[0:1, 1:2]),
                lambda: scalar.activation(out=rows[7][:, 0:tw], in_=rows[6][:, 0:tw], func=AF.Exp,
                                          scale=-0.5, bias=cst[0:1, 2:3]),
            ])
        rr_done = s_sc.v

        @blk.tensor
        def _(tensor, rr_done=rr_done, tw=tw):
            tensor.wait_ge(s_sc.h, rr_done)
            tensor.matmul(out=psC[:, 0:tw], lhsT=ones_f32[0:1, :], rhs=rows[7][:, 0:tw],
                          start=True, stop=True).then_inc(s_pe.h, 1)
        s_pe.inc(1)
        rbc_done = s_pe.v

        @blk.vector
        def _(vector, rbc_done=rbc_done, t0=t0, tw=tw, dv_now=s_dv.v):
            vector.wait_ge(s_pe.h, rbc_done)
            vector.wait_ge(s_dv.h, dv_now)
            fns = [lambda: vector.tensor_copy(out=rb[:, 0:tw], in_=psC[:, 0:tw])]
            for c in range(16):
                fns.append(lambda c=c: vector.tensor_tensor(
                    out=tmp[:, 0:tw], in0=y[c][:, 0:tw], in1=rb[:, 0:tw], op=ALU.mult))
                fns.append(lambda c=c: vector.tensor_tensor(
                    out=x[c][:, t0:t0 + tw], in0=x[c][:, t0:t0 + tw], in1=tmp[:, 0:tw], op=ALU.add))
            chain(vector, s_dv, fns)
    return s_dv.v


def build_ffn(T, with_attn_add):
    nc = bass.Bass("TRN2", target_bir_lowering=False)
    x_in = nc.dram_tensor("x_in", [D, T], F32, kind="ExternalInput").ap()
    wgu = nc.dram_tensor("wgu", [88, 128, 2048], BF16, kind="ExternalInput").ap()
    wdn = nc.dram_tensor("wdn", [16, 128, NI], BF16, kind="ExternalInput").ap()
    attn_in = None
    if with_attn_add:
        attn_in = nc.dram_tensor("attn_in", [D, T], F32, kind="ExternalInput").ap()
    x_out = nc.dram_tensor("x_out", [D, T], F32, kind="ExternalOutput").ap()

    with ExitStack() as ctx:
        E = ctx.enter_context
        BLK = BlkMgr(nc)
        bufs = alloc_common(nc, E, T, with_h=True)
        bufs["wgc"] = [E(nc.sbuf_tensor(f"wgc{i}", [128, 2048], BF16)) for i in range(2)]
        bufs["wuc"] = [E(nc.sbuf_tensor(f"wuc{i}", [128, 2048], BF16)) for i in range(2)]
        bufs["wdc"] = [E(nc.sbuf_tensor(f"wdc{i}", [128, NI], BF16)) for i in range(2)]
        bufs["sg_bf"] = E(nc.sbuf_tensor("sg_bf", [128, 512], BF16))
        bufs["sqy"] = E(nc.sbuf_tensor("sqy", [128, 512], BF16))
        bufs["rb"] = E(nc.sbuf_tensor("rb", [128, 512], F32))
        atmp = E(nc.sbuf_tensor("atmp", [128, 512], F32)) if with_attn_add else None
        s_ld, s_ld2, s_pe, s_dv, s_sc, s_st = _mk(
            nc, ctx, ["s_ld", "s_ld2", "s_pe", "s_dv", "s_sc", "s_st"])

        blk = BLK.new("load")

        @blk.sync
        def _(sync):
            for c in range(16):
                sync.dma_start(out=bufs["x"][c][:], in_=x_in[c * 128:(c + 1) * 128, :]).then_inc(s_ld.h, 16)
        s_ld.inc(16 * 16)
        x_loaded = s_ld.v

        if with_attn_add:
            for c in range(16):
                for hf in range(T // 512):
                    blk = BLK.new(f"aadd{c}_{hf}")

                    @blk.gpsimd
                    def _(gpsimd, c=c, hf=hf, dv_now=s_dv.v):
                        if c + hf >= 1:
                            gpsimd.wait_ge(s_dv.h, dv_now)
                        gpsimd.dma_start(out=atmp[:],
                                         in_=attn_in[c * 128:(c + 1) * 128, hf * 512:(hf + 1) * 512]
                                         ).then_inc(s_ld2.h, 16)
                    s_ld2.inc(16)
                    a_loaded = s_ld2.v

                    @blk.vector
                    def _(vector, c=c, hf=hf, a_loaded=a_loaded, x_loaded=x_loaded):
                        vector.wait_ge(s_ld2.h, a_loaded)
                        if c == 0 and hf == 0:
                            vector.wait_ge(s_ld.h, x_loaded)
                        vector.tensor_tensor(out=bufs["x"][c][:, hf * 512:(hf + 1) * 512],
                                             in0=bufs["x"][c][:, hf * 512:(hf + 1) * 512],
                                             in1=atmp[:], op=ALU.add).then_inc(s_dv.h, 1)
                    s_dv.inc(1)

        blk = BLK.new("init")

        @blk.vector
        def _(vector, x_loaded=x_loaded, dv_now=s_dv.v):
            if not with_attn_add:
                vector.wait_ge(s_ld.h, x_loaded)
            vector.memset(bufs["ones_f32"][:], 1.0)
            vector.memset(bufs["ones_bf"][:], 1.0)
            vector.memset(bufs["cst"][0:1, 0:1], LN_EPS)
            vector.memset(bufs["cst"][0:1, 1:2], RMS_EPS)
            vector.memset(bufs["cst"][0:1, 2:3], float(np.log(0.5))).then_inc(s_dv.h, 1)
        s_dv.inc(1)

        emit_ffn_body(nc, BLK, bufs, (s_ld, s_ld2, s_pe, s_dv, s_sc), T, wgu, wdn, "f")
        done = s_dv.v

        blk = BLK.new("store")

        @blk.gpsimd
        def _(gpsimd, done=done):
            gpsimd.wait_ge(s_dv.h, done)
            for c in range(16):
                gpsimd.dma_start(out=x_out[c * 128:(c + 1) * 128, :], in_=bufs["x"][c][:]).then_inc(s_st.h, 16)
            gpsimd.wait_ge(s_st.h, 16 * 16)
        s_st.inc(16 * 16)
        BLK.close()
    return nc



def build_qkv(T):
    nc = bass.Bass("TRN2", target_bir_lowering=False)
    x_in = nc.dram_tensor("x_in", [D, T], F32, kind="ExternalInput").ap()
    wqk = nc.dram_tensor("wqk", [32, 128, 2048], F32, kind="ExternalInput").ap()
    wv = nc.dram_tensor("wv", [8, 128, 4096], F32, kind="ExternalInput").ap()
    bqk = nc.dram_tensor("bqk", [128, 32], F32, kind="ExternalInput").ap()
    bv = nc.dram_tensor("bv", [1, D], F32, kind="ExternalInput").ap()
    tab_in = {nm: nc.dram_tensor(nm, [128, T], F32, kind="ExternalInput").ap()
              for nm in ("cosq", "sinq", "cosk", "sink")}
    q_out = nc.dram_tensor("q_out", [NH, 128, T], BF16, kind="ExternalOutput").ap()
    k_out = nc.dram_tensor("k_out", [NH, 128, T], BF16, kind="ExternalOutput").ap()
    v_out = nc.dram_tensor("v_out", [NH, T, 128], BF16, kind="ExternalOutput").ap()

    with ExitStack() as ctx:
        E = ctx.enter_context
        BLK = BlkMgr(nc)
        bufs = alloc_common(nc, E, T, with_h=False)
        psA, psB = bufs["psA"], bufs["psB"]
        xn = bufs["xn"]
        wcol = E(nc.sbuf_tensor("wcol", [128, 2048], F32))
        wcol_bf = [E(nc.sbuf_tensor(f"wcolb{i}", [128, 2048], BF16)) for i in range(2)]
        bqk_sb = E(nc.sbuf_tensor("bqk_sb", [128, 32], F32))
        bvb = E(nc.sbuf_tensor("bvb", [128, D], BF16))
        tabs = {nm: E(nc.sbuf_tensor(f"t_{nm}", [128, T], F32)) for nm in tab_in}
        qc = E(nc.sbuf_tensor("qc", [128, 512], F32))
        t2 = E(nc.sbuf_tensor("t2", [128, 512], F32))
        qb = E(nc.sbuf_tensor("qb", [128, 512], F32))
        qk_bf = [E(nc.sbuf_tensor(f"qk_bf{i}", [128, 512], BF16)) for i in range(2)]
        v_bf = [E(nc.sbuf_tensor(f"v_bf{i}", [128, 256], BF16)) for i in range(2)]
        wvb = E(nc.sbuf_tensor("wvb", [128, 4096], BF16))
        s_ld, s_ld2, s_pe, s_dv, s_sc, s_st = _mk(
            nc, ctx, ["s_ld", "s_ld2", "s_pe", "s_dv", "s_sc", "s_st"])

        blk = BLK.new("load")

        @blk.sync
        def _(sync):
            for c in range(16):
                sync.dma_start(out=bufs["x"][c][:], in_=x_in[c * 128:(c + 1) * 128, :]).then_inc(s_ld.h, 16)
            sync.dma_start(out=bqk_sb[:], in_=bqk).then_inc(s_ld.h, 16)
            for nm in tabs:
                sync.dma_start(out=tabs[nm][:], in_=tab_in[nm]).then_inc(s_ld.h, 16)
        s_ld.inc(16 * 21)

        @blk.gpsimd
        def _(gpsimd):
            gpsimd.dma_start(out=bvb[:], in_=bv.broadcast_to([128, D])).then_inc(s_ld2.h, 16)
        s_ld2.inc(16)
        x_loaded = s_ld.v

        blk = BLK.new("init")

        @blk.vector
        def _(vector):
            vector.wait_ge(s_ld.h, x_loaded)
            vector.wait_ge(s_ld2.h, 16)
            vector.memset(bufs["ones_f32"][:], 1.0)
            vector.memset(bufs["ones_bf"][:], 1.0)
            vector.memset(bufs["cst"][0:1, 0:1], LN_EPS)
            vector.memset(bufs["cst"][0:1, 1:2], RMS_EPS)
            vector.memset(bufs["cst"][0:1, 2:3], float(np.log(0.5))).then_inc(s_dv.h, 1)
        s_dv.inc(1)

        for tt in range((T + 511) // 512):
            emit_ln(nc, BLK, bufs, (s_pe, s_dv, s_sc), tt * 512, min(512, T - tt * 512), f"q{tt}")

        for j in range(32):
            p = j % 2
            h_idx = j // 2
            is_q = (j % 2) == 0
            blk = BLK.new(f"qk{j}")

            @blk.sync
            def _(sync, j=j, dv_now=s_dv.v):
                if j >= 1:
                    sync.wait_ge(s_dv.h, dv_now)
                sync.dma_start(out=wcol[:], in_=wqk[j]).then_inc(s_ld.h, 16)
            s_ld.inc(16)
            w_loaded = s_ld.v

            @blk.vector
            def _(vector, p=p, w_loaded=w_loaded, pe_now=s_pe.v):
                vector.wait_ge(s_ld.h, w_loaded)
                vector.wait_ge(s_pe.h, pe_now)
                vector.tensor_copy(out=wcol_bf[p][:], in_=wcol[:]).then_inc(s_dv.h, 1)
            s_dv.inc(1)
            w_cast = s_dv.v

            for tt in range((T + 511) // 512):
                t0 = tt * 512
                tw = min(512, T - t0)
                ps = psA if tt % 2 == 0 else psB
                op = (j * 2 + tt) % 2
                blk = BLK.new(f"qk{j}_{tt}")

                @blk.tensor
                def _(tensor, p=p, ps=ps, t0=t0, tw=tw, w_cast=w_cast, dv_now=s_dv.v):
                    tensor.wait_ge(s_dv.h, w_cast)
                    tensor.wait_ge(s_dv.h, dv_now)
                    for g in range(16):
                        ins = tensor.matmul(out=ps[:, 0:tw], lhsT=wcol_bf[p][:, g * 128:(g + 1) * 128],
                                            rhs=xn[g][:, t0:t0 + tw], start=(g == 0), stop=(g == 15))
                        if g == 15:
                            ins.then_inc(s_pe.h, 1)
                s_pe.inc(1)
                mm_done = s_pe.v

                @blk.scalar
                def _(scalar, j=j, ps=ps, tw=tw, mm_done=mm_done, dv_now=s_dv.v):
                    scalar.wait_ge(s_pe.h, mm_done)
                    scalar.wait_ge(s_dv.h, dv_now)
                    scalar.activation(out=qb[:, 0:tw], in_=ps[:, 0:tw], func=AF.Identity,
                                      bias=bqk_sb[:, j:j + 1]).then_inc(s_sc.h, 1)
                s_sc.inc(1)
                qb_done = s_sc.v

                @blk.vector
                def _(vector, is_q=is_q, t0=t0, tw=tw, qb_done=qb_done, op=op, st_now=s_st.v):
                    vector.wait_ge(s_sc.h, qb_done)
                    if st_now >= 16:
                        vector.wait_ge(s_st.h, st_now - 16 + 16)
                    cos_t = tabs["cosq"] if is_q else tabs["cosk"]
                    sin_t = tabs["sinq"] if is_q else tabs["sink"]
                    chain(vector, s_dv, [
                        lambda: vector.tensor_copy(out=t2[0:64, 0:tw], in_=qb[64:128, 0:tw]),
                        lambda: vector.tensor_copy(out=t2[64:128, 0:tw], in_=qb[0:64, 0:tw]),
                        lambda: vector.tensor_tensor(out=qc[:, 0:tw], in0=qb[:, 0:tw],
                                                     in1=cos_t[:, t0:t0 + tw], op=ALU.mult),
                        lambda: vector.tensor_tensor(out=t2[:, 0:tw], in0=t2[:, 0:tw],
                                                     in1=sin_t[:, t0:t0 + tw], op=ALU.mult),
                        lambda: vector.tensor_tensor(out=qk_bf[op][:, 0:tw], in0=qc[:, 0:tw],
                                                     in1=t2[:, 0:tw], op=ALU.add),
                    ])
                rope_done = s_dv.v

                @blk.gpsimd
                def _(gpsimd, h_idx=h_idx, is_q=is_q, t0=t0, tw=tw, rope_done=rope_done, op=op):
                    gpsimd.wait_ge(s_dv.h, rope_done)
                    dst = q_out if is_q else k_out
                    gpsimd.dma_start(out=dst[h_idx, :, t0:t0 + tw], in_=qk_bf[op][:, 0:tw]).then_inc(s_st.h, 16)
                s_st.inc(16)

        for fc in range(8):
            blk = BLK.new(f"v{fc}_ld")

            for hf in range(2):
                @blk.sync
                def _(sync, fc=fc, hf=hf, dv_now=s_dv.v):
                    sync.wait_ge(s_dv.h, dv_now)
                    sync.dma_start(out=wcol[:], in_=wv[fc, :, hf * 2048:(hf + 1) * 2048]).then_inc(s_ld.h, 16)
                s_ld.inc(16)
                wv_loaded = s_ld.v

                @blk.vector
                def _(vector, hf=hf, wv_loaded=wv_loaded, pe_now=s_pe.v):
                    vector.wait_ge(s_ld.h, wv_loaded)
                    vector.wait_ge(s_pe.h, pe_now)
                    vector.tensor_copy(out=wvb[:, hf * 2048:(hf + 1) * 2048],
                                       in_=wcol[:]).then_inc(s_dv.h, 1)
                s_dv.inc(1)
            v_cast = s_dv.v

            for t in range(T // 128):
                ps = psA if t % 2 == 0 else psB
                op = t % 2
                blk = BLK.new(f"v{fc}_{t}")

                @blk.tensor
                def _(tensor, fc=fc, t=t, ps=ps, v_cast=v_cast, dv_now=s_dv.v):
                    tensor.wait_ge(s_dv.h, v_cast)
                    tensor.wait_ge(s_dv.h, dv_now)
                    for g in range(16):
                        ins = tensor.matmul(out=ps[:, 0:256],
                                            lhsT=xn[g][:, t * 128:(t + 1) * 128],
                                            rhs=wvb[:, g * 256:(g + 1) * 256],
                                            start=(g == 0), stop=(g == 15))
                        if g == 15:
                            ins.then_inc(s_pe.h, 1)
                s_pe.inc(1)
                vmm_done = s_pe.v

                @blk.vector
                def _(vector, fc=fc, ps=ps, vmm_done=vmm_done, op=op, st_now=s_st.v):
                    vector.wait_ge(s_pe.h, vmm_done)
                    if st_now >= 32:
                        vector.wait_ge(s_st.h, st_now - 32 + 32)
                    vector.tensor_tensor(out=v_bf[op][:, 0:256], in0=ps[:, 0:256],
                                         in1=bvb[:, fc * 256:(fc + 1) * 256], op=ALU.add).then_inc(s_dv.h, 1)
                s_dv.inc(1)
                vevac = s_dv.v

                @blk.gpsimd
                def _(gpsimd, fc=fc, t=t, vevac=vevac, op=op):
                    gpsimd.wait_ge(s_dv.h, vevac)
                    for s in range(2):
                        h_i = fc * 2 + s
                        gpsimd.dma_start(out=v_out[h_i, t * 128:(t + 1) * 128, :],
                                         in_=v_bf[op][:, s * 128:(s + 1) * 128]).then_inc(s_st.h, 16)
                s_st.inc(32)

        blk = BLK.new("fin")
        final = s_st.v

        @blk.gpsimd
        def _(gpsimd):
            gpsimd.wait_ge(s_st.h, final)
        BLK.close()
    return nc



def build_attn(nbh):
    nc = bass.Bass("TRN2", target_bir_lowering=False)
    q_in = nc.dram_tensor("q_in", [nbh, 128, SEQ], BF16, kind="ExternalInput").ap()
    k_in = nc.dram_tensor("k_in", [nbh, 128, SEQ], BF16, kind="ExternalInput").ap()
    v_in = nc.dram_tensor("v_in", [nbh, 128, SEQ], BF16, kind="ExternalInput").ap()
    mask_in = nc.dram_tensor("mask_in", [128, 128], BF16, kind="ExternalInput").ap()
    a_out = nc.dram_tensor("a_out", [nbh, 128, SEQ], F32, kind="ExternalOutput").ap()

    NQ = SEQ // 512

    with ExitStack() as ctx:
        E = ctx.enter_context
        BLK = BlkMgr(nc)
        q_sb = E(nc.sbuf_tensor("q_sb", [128, SEQ], BF16))
        k_sb = E(nc.sbuf_tensor("k_sb", [128, SEQ], BF16))
        v_sb = E(nc.sbuf_tensor("v_sb", [128, SEQ], BF16))
        msk = E(nc.sbuf_tensor("msk", [128, 128], BF16))
        pt = [E(nc.sbuf_tensor(f"pt{i}", [128, 512], BF16)) for i in range(2)]
        ones_bf = E(nc.sbuf_tensor("ones_bf", [128, 1], BF16))
        ones_f32 = E(nc.sbuf_tensor("ones_f32", [1, 128], F32))
        r_sb = E(nc.sbuf_tensor("r_sb", [1, 512], F32))
        rb_sb = E(nc.sbuf_tensor("rb_sb", [128, 512], F32))
        o_sb = [E(nc.sbuf_tensor(f"o_sb{i}", [128, 512], F32)) for i in range(2)]
        ps_st = [E(nc.psum_tensor(f"ps_st{i}", [128, 512], F32)) for i in range(2)]
        ps_pv = E(nc.psum_tensor("ps_pv", [128, 512], F32))
        ps_l = E(nc.psum_tensor("ps_l", [1, 512], F32))
        ps_rb = E(nc.psum_tensor("ps_rb", [128, 512], F32))
        s_ld, s_pe, s_dv, s_sc, s_st = _mk(nc, ctx, ["s_ld", "s_pe", "s_dv", "s_sc", "s_st"])

        blk = BLK.new("init")

        @blk.vector
        def _(vector):
            vector.memset(ones_bf[:], 1.0)
            vector.memset(ones_f32[:], 1.0).then_inc(s_dv.h, 1)
        s_dv.inc(1)
        init_done = s_dv.v

        @blk.sync
        def _(sync):
            sync.dma_start(out=msk[:], in_=mask_in).then_inc(s_ld.h, 16)
        s_ld.inc(16)

        last_exp = {0: 0, 1: 0}
        last_lpv = {0: 0, 1: 0}
        prev_norm_dv = 0

        for bh in range(nbh):
            blk = BLK.new(f"bh{bh}_ld")

            @blk.sync
            def _(sync, bh=bh, st_now=s_st.v):
                if bh > 0:
                    sync.wait_ge(s_st.h, st_now)
                sync.dma_start(out=q_sb[:], in_=q_in[bh]).then_inc(s_ld.h, 16)
                sync.dma_start(out=k_sb[:], in_=k_in[bh]).then_inc(s_ld.h, 16)
                sync.dma_start(out=v_sb[:], in_=v_in[bh]).then_inc(s_ld.h, 16)
            s_ld.inc(48)
            kv_loaded = s_ld.v

            for qb in range(NQ):
                n_kt = 4 * qb + 4
                for kt in range(n_kt):
                    par = kt % 2
                    scol0 = max(0, (kt - qb * 4)) * 128
                    diag = kt >= qb * 4
                    blk = BLK.new(f"b{bh}q{qb}k{kt}")

                    @blk.tensor
                    def _(tensor, qb=qb, kt=kt, par=par, scol0=scol0,
                          kv_loaded=kv_loaded, le=last_exp[par], first=(qb == 0 and kt == 0)):
                        if first:
                            tensor.wait_ge(s_ld.h, kv_loaded)
                            tensor.wait_ge(s_dv.h, init_done)
                        if le > 0:
                            tensor.wait_ge(s_sc.h, le)
                        tensor.matmul(out=ps_st[par][:, scol0:512],
                                      lhsT=k_sb[:, kt * 128:(kt + 1) * 128],
                                      rhs=q_sb[:, qb * 512 + scol0:qb * 512 + 512],
                                      start=True, stop=True).then_inc(s_pe.h, 1)
                    s_pe.inc(1)
                    st_done = s_pe.v

                    @blk.scalar
                    def _(scalar, par=par, scol0=scol0, st_done=st_done, lp=last_lpv[par]):
                        scalar.wait_ge(s_pe.h, st_done)
                        if lp > 0:
                            scalar.wait_ge(s_pe.h, lp)
                        scalar.activation(out=pt[par][:, scol0:512], in_=ps_st[par][:, scol0:512],
                                          func=AF.Exp).then_inc(s_sc.h, 1)
                    s_sc.inc(1)
                    last_exp[par] = s_sc.v
                    exp_done = s_sc.v

                    if diag:
                        @blk.vector
                        def _(vector, par=par, scol0=scol0, exp_done=exp_done):
                            vector.wait_ge(s_sc.h, exp_done)
                            vector.tensor_tensor(out=pt[par][:, scol0:scol0 + 128],
                                                 in0=pt[par][:, scol0:scol0 + 128],
                                                 in1=msk[:], op=ALU.mult).then_inc(s_dv.h, 1)
                        s_dv.inc(1)
                        rsem, rval = s_dv, s_dv.v
                    else:
                        rsem, rval = s_sc, exp_done

                    @blk.tensor
                    def _(tensor, par=par, scol0=scol0, kt=kt, n_kt=n_kt, rsem=rsem, rval=rval,
                          pnorm=prev_norm_dv):
                        tensor.wait_ge(rsem.h, rval)
                        if kt == 0 and pnorm > 0:
                            tensor.wait_ge(s_dv.h, pnorm)
                        tensor.matmul(out=ps_l[0:1, scol0:512], lhsT=ones_bf[:, 0:1],
                                      rhs=pt[par][:, scol0:512], start=(kt == 0), stop=(kt == n_kt - 1))
                        tensor.matmul(out=ps_pv[:, scol0:512], lhsT=v_sb[:, kt * 128:(kt + 1) * 128],
                                      rhs=pt[par][:, scol0:512], start=(kt == 0),
                                      stop=(kt == n_kt - 1)).then_inc(s_pe.h, 1)
                    s_pe.inc(1)
                    last_lpv[par] = s_pe.v

                pv_done = s_pe.v
                op = qb % 2
                blk = BLK.new(f"b{bh}q{qb}n")

                @blk.vector
                def _(vector, pv_done=pv_done):
                    vector.wait_ge(s_pe.h, pv_done)
                    vector.reciprocal(out=r_sb[:], in_=ps_l[0:1, :]).then_inc(s_dv.h, 1)
                s_dv.inc(1)
                r_done = s_dv.v

                @blk.tensor
                def _(tensor, r_done=r_done):
                    tensor.wait_ge(s_dv.h, r_done)
                    tensor.matmul(out=ps_rb[:], lhsT=ones_f32[:], rhs=r_sb[:],
                                  start=True, stop=True).then_inc(s_pe.h, 1)
                s_pe.inc(1)
                rb_done = s_pe.v

                @blk.vector
                def _(vector, op=op, rb_done=rb_done, st_now=s_st.v):
                    vector.wait_ge(s_pe.h, rb_done)
                    if st_now >= 32:
                        vector.wait_ge(s_st.h, st_now - 32 + 32)
                    chain(vector, s_dv, [
                        lambda: vector.tensor_copy(out=rb_sb[:], in_=ps_rb[:]),
                        lambda: vector.tensor_tensor(out=o_sb[op][:], in0=ps_pv[:],
                                                     in1=rb_sb[:], op=ALU.mult),
                    ])
                o_done = s_dv.v
                prev_norm_dv = o_done

                @blk.gpsimd
                def _(gpsimd, bh=bh, qb=qb, op=op, o_done=o_done):
                    gpsimd.wait_ge(s_dv.h, o_done)
                    gpsimd.dma_start(out=a_out[bh, :, qb * 512:(qb + 1) * 512],
                                     in_=o_sb[op][:]).then_inc(s_st.h, 16)
                s_st.inc(16)

        blk = BLK.new("fin")
        final = s_st.v

        @blk.gpsimd
        def _(gpsimd):
            gpsimd.wait_ge(s_st.h, final)
        BLK.close()
    return nc



_G128 = np.concatenate([np.arange(0, 128, 2), np.arange(1, 128, 2)])


def _permuted_rows(n):
    a = np.arange(n)
    return (a // 128) * 128 + _G128[a % 128]


def _rope_tables():
    inv = 1.0 / (10000.0 ** (np.arange(0, HD, 2, dtype=np.float64) / HD))
    t = np.arange(SEQ, dtype=np.float64)
    fr = np.outer(t, inv)
    emb = np.concatenate([fr, fr], axis=1)
    cos = np.cos(emb).astype(np.float32).T
    sin = np.sin(emb).astype(np.float32).T
    sinA = sin.copy()
    sinA[0:64] = -sin[0:64]
    return cos, sinA


def kernel(L0_attn_hidden_states, L0_attn_w_qkv, L0_attn_b_qkv,
           L0_ffn_gate_up_packed, L0_ffn_gate_up_scales,
           L0_ffn_down_packed, L0_ffn_down_scales):
    x = np.asarray(L0_attn_hidden_states)
    w_qkv = np.asarray(L0_attn_w_qkv)
    b_qkv = np.asarray(L0_attn_b_qkv)
    gup = np.asarray(L0_ffn_gate_up_packed)
    gus = np.asarray(L0_ffn_gate_up_scales)
    dnp = np.asarray(L0_ffn_down_packed)
    dns = np.asarray(L0_ffn_down_scales)

    cores = list(range(NC))

    gw, dw = 2 * NI // NC, D // NC
    pr_d = _permuted_rows(D)
    pr_io = _permuted_rows(2 * NI)
    pr_dm = _permuted_rows(D)
    byrow = lambda n: (np.arange(n) // 128) * 64 + (np.arange(n) % 128) % 64
    bytes_g = gup.view(np.uint8).reshape(2 * NI, 1024).T
    gb = bytes_g[byrow(D)][:, pr_io]
    bytes_d = dnp.view(np.uint8).reshape(D, 2816).T
    db = bytes_d[byrow(NI)][:, pr_dm]
    sg_p = gus[:, pr_io]
    sd_p = dns[:, pr_dm]
    nc1 = build_dequant(gw, dw)
    maps1 = [{
        "b2g": np.ascontiguousarray(gb[:, c * gw:(c + 1) * gw]),
        "sg": np.ascontiguousarray(sg_p[:, c * gw:(c + 1) * gw]),
        "b2d": np.ascontiguousarray(db[:, c * dw:(c + 1) * dw]),
        "sd": np.ascontiguousarray(sd_p[:, c * dw:(c + 1) * dw]),
    } for c in cores]
    r1 = run_bass_kernel_spmd(nc1, maps1, core_ids=cores).results
    wguT = np.concatenate([r1[c]["wg"] for c in cores], axis=1)
    wdnT = np.concatenate([r1[c]["wd"] for c in cores], axis=1)
    wgu_cols = np.ascontiguousarray(
        wguT.reshape(16, 128, 88, 128).transpose(2, 1, 0, 3).reshape(88, 128, 2048))
    wdn_cols = np.ascontiguousarray(
        wdnT.reshape(44, 128, 16, 128).transpose(2, 1, 0, 3).reshape(16, 128, NI))

    T = SEQ * NB // NC
    xT = np.ascontiguousarray(x.transpose(0, 2, 1)[:, pr_d, :])
    shards = [(c // 4, (c % 4) * T) for c in cores]

    nc2 = build_ffn(T, with_attn_add=False)
    maps2 = [{"x_in": np.ascontiguousarray(xT[b][:, t0:t0 + T]), "wgu": wgu_cols, "wdn": wdn_cols}
             for (b, t0) in shards]
    r2 = run_bass_kernel_spmd(nc2, maps2, core_ids=cores).results
    x1T = np.empty((NB, D, SEQ), np.float32)
    for c, (b, t0) in enumerate(shards):
        x1T[b][:, t0:t0 + T] = r2[c]["x_out"]

    cos, sinA = _rope_tables()
    wr = w_qkv.reshape(NH, 3, HD, D)
    wqkT = wr[:, :2].transpose(3, 0, 1, 2).reshape(D, 2 * D)[pr_d]
    wqk_cols = np.ascontiguousarray(
        wqkT.reshape(16, 128, 32, 128).transpose(2, 1, 0, 3).reshape(32, 128, 2048))
    wvT = wr[:, 2].transpose(2, 0, 1).reshape(D, D)[pr_d][:, _permuted_rows(D)]
    wv_cols = np.ascontiguousarray(
        wvT.reshape(16, 128, 8, 256).transpose(2, 1, 0, 3).reshape(8, 128, 4096))
    br = b_qkv.reshape(NH, 3, HD)
    bqk_t = np.ascontiguousarray(br[:, :2].reshape(NH * 2, HD).T)
    bv_row = np.ascontiguousarray(br[:, 2].reshape(1, D)[:, _permuted_rows(D)])
    nc3 = build_qkv(T)
    maps3 = [{
        "x_in": np.ascontiguousarray(x1T[b][:, t0:t0 + T]),
        "wqk": wqk_cols, "wv": wv_cols, "bqk": bqk_t, "bv": bv_row,
        "cosq": np.ascontiguousarray(cos[:, t0:t0 + T] * np.float32(SM_SCALE)),
        "sinq": np.ascontiguousarray(sinA[:, t0:t0 + T] * np.float32(SM_SCALE)),
        "cosk": np.ascontiguousarray(cos[:, t0:t0 + T]),
        "sink": np.ascontiguousarray(sinA[:, t0:t0 + T]),
    } for (b, t0) in shards]
    r3 = run_bass_kernel_spmd(nc3, maps3, core_ids=cores).results
    qf = np.empty((NB, NH, HD, SEQ), BF)
    kf = np.empty((NB, NH, HD, SEQ), BF)
    vf = np.empty((NB, NH, SEQ, HD), BF)
    for c, (b, t0) in enumerate(shards):
        qf[b][:, :, t0:t0 + T] = r3[c]["q_out"]
        kf[b][:, :, t0:t0 + T] = r3[c]["k_out"]
        vf[b][:, t0:t0 + T, :] = r3[c]["v_out"]

    nbh = NB * NH // NC
    maskT = np.triu(np.ones((128, 128), np.float32)).astype(BF)
    nc4 = build_attn(nbh)
    maps4 = []
    for c in cores:
        idx = [(g // NH, g % NH) for g in range(c * nbh, (c + 1) * nbh)]
        maps4.append({
            "q_in": np.ascontiguousarray(np.stack([np.asarray(qf[b][h]) for (b, h) in idx])),
            "k_in": np.ascontiguousarray(np.stack([np.asarray(kf[b][h]) for (b, h) in idx])),
            "v_in": np.ascontiguousarray(np.stack(
                [np.asarray(vf[b][h]).reshape(32, 128, 128).transpose(1, 0, 2).reshape(128, SEQ)
                 for (b, h) in idx])),
            "mask_in": maskT,
        })
    r4 = run_bass_kernel_spmd(nc4, maps4, core_ids=cores).results
    attnT = np.empty((NB, D, SEQ), np.float32)
    for c in cores:
        idx = [(g // NH, g % NH) for g in range(c * nbh, (c + 1) * nbh)]
        for i, (b, h) in enumerate(idx):
            attnT[b][h * HD:(h + 1) * HD, :] = r4[c]["a_out"][i]

    nc5 = build_ffn(T, with_attn_add=True)
    maps5 = [{"x_in": np.ascontiguousarray(x1T[b][:, t0:t0 + T]),
              "attn_in": np.ascontiguousarray(attnT[b][:, t0:t0 + T]),
              "wgu": wgu_cols, "wdn": wdn_cols}
             for (b, t0) in shards]
    r5 = run_bass_kernel_spmd(nc5, maps5, core_ids=cores).results
    outT = np.empty((NB, D, SEQ), np.float32)
    for c, (b, t0) in enumerate(shards):
        outT[b][:, t0:t0 + T] = r5[c]["x_out"]
    un = np.empty((NB, D, SEQ), np.float32)
    un[:, pr_d, :] = outT
    return np.ascontiguousarray(un.transpose(0, 2, 1))
